# version 01
# baseline (speedup 1.0000x reference)
import numpy as np
from contextlib import ExitStack

import concourse.bass as bass
import concourse.mybir as mybir
from concourse.bass_utils import run_bass_kernel_spmd

F32 = mybir.dt.float32
BF16 = mybir.dt.bfloat16
I16 = mybir.dt.int16
AF = mybir.ActivationFunctionType
ALU = mybir.AluOpType
AX = mybir.AxisListType

B, S, D = 4, 4096, 2048
NH, HD = 16, 128
DC = D // 128
T_OWN, SPAN = 2048, 4096
NE, F_MOE = 16, 1024
FC_MOE = F_MOE // 128
FC_SH = 2048 // 128
HC = 16
NTB, TBS = 4, 512
RS128 = float(1.0 / np.sqrt(128.0))
BIGPOS = 32000

K_STOP_PHASE = None
K_NH = None
K_RSTOP = 4
K_NOVF = False
K_WJOBS = None
K_FF_NE = None
K_FF_SH = True

KTILES_G = [
    list(range(0, 4)) + list(range(16, 24)),
    list(range(0, 8)) + list(range(16, 24)),
    list(range(0, 12)) + list(range(16, 32)),
    list(range(0, 32)),
]
KBLKS_G = [
    [0, 4, 5],
    [0, 1, 4, 5],
    [0, 1, 2, 4, 5, 6, 7],
    [0, 1, 2, 3, 4, 5, 6, 7],
]


def build_nc():
    nc = bass.Bass("TRN2", target_bir_lowering=False)

    def din(name, shape, dt=F32):
        return nc.dram_tensor(name, list(shape), dt, kind="ExternalInput").ap()

    x_span = din("x_span", (SPAN, D))
    wq = din("wq", (D, NH, HD))
    wk = din("wk", (D, NH, HD))
    wv = din("wv", (D, NH, HD))
    wo = din("wo", (NH, HD, D))
    qpos_col = din("qpos_col", (128, 16))
    kpos_col = din("kpos_col", (128, 32))
    qpos_bc = din("qpos_bc", (128, T_OWN), I16)
    kpos_br = din("kpos_br", (128, SPAN), I16)
    cos_k = din("cos_k", (128, SPAN))
    sin_ks = din("sin_ks", (128, SPAN))
    identity = din("identity", (128, 128))
    ones_row = din("ones_row", (1, 128))
    gate_w = din("gate_w", (NE, D))
    ebias = din("ebias", (NE, 1))
    gw_moe = din("gw_moe", (NE, F_MOE, D))
    uw_moe = din("uw_moe", (NE, F_MOE, D))
    dw_moe = din("dw_moe", (NE, D, F_MOE))
    sh_g = din("sh_g", (2048, D))
    sh_u = din("sh_u", (2048, D))
    sh_d = din("sh_d", (D, 2048))

    out = nc.dram_tensor("out", [T_OWN, D], F32, kind="ExternalOutput").ap()

    def scr(name, shape):
        return nc.dram_tensor(name, list(shape), BF16, kind="Internal").ap()

    attnT_scr = scr("attnT_scr", (NH, 128, T_OWN))
    attn_out_scr = scr("attn_out_scr", (16, 128, D))
    gwT_scr = scr("gwT_scr", (NE, FC_MOE, 128, DC * 128))
    uwT_scr = scr("uwT_scr", (NE, FC_MOE, 128, DC * 128))
    dwT_scr = scr("dwT_scr", (NE, 2, FC_MOE, 128, 8, 128))
    shgT_scr = scr("shgT_scr", (FC_SH, 128, DC * 128))
    shuT_scr = scr("shuT_scr", (FC_SH, 128, DC * 128))
    shdT_scr = scr("shdT_scr", (2, FC_SH, 128, 8, 128))

    es = ExitStack()
    with es:
        class Ch:
            __slots__ = ("h", "n")

            def __init__(self, name):
                self.h = es.enter_context(nc.semaphore(name))
                self.n = 0

            def inc(self, ins, k=1):
                ins.then_inc(self.h, k)
                self.n += k
                return self.n

            def wait(self, e, n=None):
                n = self.n if n is None else n
                if n > 0:
                    e.wait_ge(self.h, n)

        PE, VE, SE, GE = Ch("pe"), Ch("ve"), Ch("se"), Ch("ge")
        DX, DW, DS = Ch("dx"), Ch("dw"), Ch("ds")
        BAR = Ch("bar")

        blk = es.enter_context(nc.Block())

        def on(engname, fn):
            getattr(blk, engname)(fn)

        ENGS = ["tensor", "vector", "scalar", "gpsimd", "sync"]

        def barrier():
            for en, ch in (("vector", VE), ("scalar", SE), ("gpsimd", GE)):
                def _(e, ch=ch):
                    e.drain().then_inc(ch.h, 1)
                on(en, _)
            VE.n += 1
            SE.n += 1
            GE.n += 1
            for en in ENGS:
                def _(e):
                    for ch in (PE, VE, SE, GE, DX, DW, DS):
                        ch.wait(e)
                on(en, _)

        def vf(e, ins):
            if K_NOVF:
                return
            ins.then_inc(VE.h, 1)
            VE.n += 1
            e.wait_ge(VE.h, VE.n)

        g_idbf = es.enter_context(nc.sbuf_tensor("g_idbf", [128, 128], BF16))
        g_idf32 = es.enter_context(nc.sbuf_tensor("g_idf32", [128, 128], F32))
        g_ones = es.enter_context(nc.sbuf_tensor("g_ones", [1, 128], BF16))
        g_qpc = es.enter_context(nc.sbuf_tensor("g_qpc", [128, 16], F32))
        g_kpc = es.enter_context(nc.sbuf_tensor("g_kpc", [128, 32], F32))
        g_w = es.enter_context(nc.sbuf_tensor("g_w", [128, 16, NE], F32))
        g_xnT = es.enter_context(nc.sbuf_tensor("g_xnT", [128, DC, T_OWN], BF16))

        def _(e):
            e.dma_start(g_idbf[:, :], identity).then_inc(DW.h, 16)
            e.dma_start(g_ones[:, :], ones_row).then_inc(DW.h, 16)
            DW.n += 32
        on("gpsimd", _)
        tok_g0_dw = DW.n

        def _(e):
            e.dma_start(g_idf32[:, :], identity).then_inc(DX.h, 16)
            e.dma_start(g_qpc[:, :], qpos_col).then_inc(DX.h, 16)
            e.dma_start(g_kpc[:, :], kpos_col).then_inc(DX.h, 16)
            DX.n += 48
        on("sync", _)
        tok_g0_dx = DX.n

        tok_attn_stores = 0
        tok_oproj_stores = 0

        with ExitStack() as ascope:
            xnT_for = ascope.enter_context(
                nc.sbuf_tensor("xnT_for", [128, DC, T_OWN], BF16))

            def xnT_blk(dc, tb):
                if tb < 4:
                    return g_xnT[:, dc, tb * 512:tb * 512 + 512]
                return xnT_for[:, dc, (tb - 4) * 512:(tb - 4) * 512 + 512]

            with ExitStack() as p1:
                xb = [p1.enter_context(nc.sbuf_tensor(f"xb{i}", [128, D], F32))
                      for i in range(2)]
                dump = p1.enter_context(nc.sbuf_tensor("dump", [128, D], F32))
                xnt = [p1.enter_context(nc.sbuf_tensor(f"xnt{i}", [128, D], BF16))
                       for i in range(2)]
                lnS = p1.enter_context(nc.sbuf_tensor("lnS", [128, 8], F32))
                ps1 = [p1.enter_context(nc.psum_tensor(f"ps1_{i}", [128, 1024], BF16))
                       for i in range(2)]

                xt = x_span.rearrange("(j p) d -> j p d", p=128)
                xb_free = [(None, 0), (None, 0)]
                xnt_free = [(None, 0), (None, 0)]
                ps_free = (None, 0)
                prev_tok_var = 0
                for j in range(32):
                    bi = j % 2

                    def _(e, j=j, bi=bi, rel=xb_free[bi]):
                        if rel[0] is not None:
                            rel[0].wait(e, rel[1])
                        e.dma_start(xb[bi][:, :], xt[j]).then_inc(DX.h, 16)
                        DX.n += 16
                    on("sync", _)
                    tok_x = DX.n

                    def _(e, bi=bi, t=tok_x):
                        DX.wait(e, t)
                        e.reduce_sum(lnS[:, 0:1], xb[bi][:, :], AX.X)
                        vf(e, e.tensor_mul(dump[:, :], xb[bi][:, :], xb[bi][:, :]))
                        vf(e, e.reduce_sum(lnS[:, 1:2], dump[:, :], AX.X))
                        e.tensor_scalar(lnS[:, 2:3], lnS[:, 0:1], 1.0 / D, None, ALU.mult)
                        vf(e, e.tensor_scalar(lnS[:, 3:4], lnS[:, 1:2], 1.0 / D, None,
                                              ALU.mult))
                        vf(e, e.tensor_scalar(lnS[:, 4:5], lnS[:, 2:3], lnS[:, 2:3],
                                              -1e-5, ALU.mult, ALU.add))
                        e.tensor_sub(lnS[:, 5:6], lnS[:, 3:4], lnS[:, 4:5])
                        e.drain().then_inc(VE.h, 1)
                        VE.n += 1
                    on("vector", _)
                    tok_var = VE.n
                    prev_tok_var = tok_var

                    def _(e, tv=tok_var):
                        VE.wait(e, tv)
                        e.activation(lnS[:, 6:7], lnS[:, 5:6], AF.Ln)
                        e.activation(lnS[:, 7:8], lnS[:, 6:7], AF.Exp, scale=-0.5)
                        e.drain().then_inc(SE.h, 1)
                        SE.n += 1
                    on("scalar", _)
                    tok_rstd = SE.n

                    def _(e, bi=bi, tr=tok_rstd, rel=xnt_free[bi]):
                        SE.wait(e, tr)
                        if rel[0] is not None:
                            rel[0].wait(e, rel[1])
                        e.tensor_scalar(xnt[bi][:, :], xb[bi][:, :], lnS[:, 2:3],
                                        lnS[:, 7:8], ALU.subtract, ALU.mult)
                        e.drain().then_inc(VE.h, 1)
                        VE.n += 1
                    on("vector", _)
                    tok_xn = VE.n
                    xb_free[bi] = (VE, tok_xn)

                    def _(e, bi=bi, txn=tok_xn, rel=ps_free, tg=tok_g0_dw):
                        VE.wait(e, txn)
                        if rel[0] is not None:
                            rel[0].wait(e, rel[1])
                        DW.wait(e, tg)
                        last = None
                        for dc in range(16):
                            bank = ps1[dc // 8]
                            last = e.transpose(
                                bank[:, (dc % 8) * 128:(dc % 8) * 128 + 128],
                                xnt[bi][:, dc * 128:dc * 128 + 128],
                                g_idbf[:, :])
                        last.then_inc(PE.h, 1)
                        PE.n += 1
                    on("tensor", _)
                    tok_tr = PE.n
                    xnt_free[bi] = (PE, tok_tr)

                    dst = g_xnT if j < 16 else xnT_for
                    col = (j % 16) * 128

                    def _(e, ttr=tok_tr, dst=dst, col=col):
                        PE.wait(e, ttr)
                        e.tensor_copy(dst[:, 0:8, col:col + 128], ps1[0][:, :])
                        e.tensor_copy(dst[:, 8:16, col:col + 128], ps1[1][:, :])
                        e.drain().then_inc(VE.h, 1)
                        VE.n += 1
                    on("vector", _)
                    ps_free = (VE, VE.n)
            barrier()

            if K_STOP_PHASE == 1:
                def _(e):
                    VE.wait(e)
                    e.dma_start(out.rearrange("(c p) t -> p c t", p=128),
                                g_xnT[:, :, :]).then_inc(DW.h, 16)
                    DW.n += 16
                    e.wait_ge(DW.h, DW.n)
                on("gpsimd", _)
                return nc

            with ExitStack() as p2:
                def sb2(name, shape, dt):
                    return p2.enter_context(nc.sbuf_tensor(name, list(shape), dt))
                t_cos = sb2("t_cos", (128, SPAN), BF16)
                t_sin = sb2("t_sin", (128, SPAN), BF16)
                t_qbc = sb2("t_qbc", (128, T_OWN), I16)
                t_kbr = sb2("t_kbr", (128, SPAN), I16)
                w_q = sb2("w_q", (128, DC, HD), BF16)
                w_k = sb2("w_k", (128, DC, HD), BF16)
                w_v = sb2("w_v", (128, DC, HD), BF16)
                Kr = sb2("Kr", (128, SPAN), BF16)
                Ksw = sb2("Ksw", (128, SPAN), BF16)
                Vv = sb2("Vv", (128, 32, HD), BF16)
                Qr = sb2("Qr", (128, T_OWN), BF16)
                PT = [sb2(f"PT{i}", (128, 512), BF16) for i in range(2)]
                Pq = [sb2(f"Pq{i}", (128, 512), BF16) for i in range(2)]
                sums = sb2("sums", (128, 8), F32)
                scol = sb2("scol", (128, 2), F32)
                rs_bf = sb2("rs_bf", (128, 1), BF16)
                rs_row = sb2("rs_row", (1, 512), BF16)
                B_sb = sb2("B_sb", (128, 512), BF16)
                aev = [sb2(f"aev{i}", (128, 512), BF16) for i in range(2)]

                psP = [p2.enter_context(nc.psum_tensor(f"psP{i}", [128, 512], F32))
                       for i in range(2)]
                psL = [p2.enter_context(nc.psum_tensor(f"psL{i}", [128, 512], F32))
                       for i in range(2)]
                psA = [p2.enter_context(nc.psum_tensor(f"psA{i}", [128, 512], F32))
                       for i in range(2)]
                psB = p2.enter_context(nc.psum_tensor("psB", [128, 512], F32))
                psMix = p2.enter_context(nc.psum_tensor("psMix", [128, 1024], BF16))

                def _(e):
                    e.dma_start(t_cos[:, :], cos_k).then_inc(DW.h, 16)
                    e.dma_start(t_sin[:, :], sin_ks).then_inc(DW.h, 16)
                    DW.n += 32
                on("gpsimd", _)
                tok_tab_dw = DW.n

                def _(e):
                    e.dma_start(t_qbc[:, :], qpos_bc).then_inc(DX.h, 16)
                    e.dma_start(t_kbr[:, :], kpos_br).then_inc(DX.h, 16)
                    DX.n += 32
                on("sync", _)
                tok_tab_dx = DX.n

                wq_r = wq.rearrange("(c p) n h -> n p c h", p=128)
                wk_r = wk.rearrange("(c p) n h -> n p c h", p=128)
                wv_r = wv.rearrange("(c p) n h -> n p c h", p=128)

                tok_w = [0] * NH

                def _(e):
                    e.dma_start(w_k[:, :, :], wk_r[0]).then_inc(DW.h, 16)
                    e.dma_start(w_q[:, :, :], wq_r[0]).then_inc(DW.h, 16)
                    e.dma_start(w_v[:, :, :], wv_r[0]).then_inc(DW.h, 16)
                    DW.n += 48
                on("gpsimd", _)
                tok_w[0] = DW.n

                prev_ev = 0
                prev_st = 0
                prev_B_se = 0
                kr_free = 0

                for h in range(K_NH or NH):
                    def proj(wbuf, nblocks, dstbuf, first_wait_dw, first_wait_pe):
                        pe_t = [0] * nblocks
                        ev_t = [0] * nblocks
                        for tb in range(nblocks):
                            def _(e, tb=tb, wbuf=wbuf, fdw=first_wait_dw,
                                  fpe=first_wait_pe):
                                if tb == 0:
                                    if fdw:
                                        DW.wait(e, fdw)
                                    if fpe:
                                        VE.wait(e, fpe)
                                if tb >= 2:
                                    VE.wait(e, ev_t[tb - 2])
                                bank = psP[tb % 2]
                                last = None
                                for dc in range(DC):
                                    last = e.matmul(bank[:, :], wbuf[:, dc, :],
                                                    xnT_blk(dc, tb),
                                                    start=(dc == 0),
                                                    stop=(dc == DC - 1))
                                last.then_inc(PE.h, 1)
                                PE.n += 1
                                pe_t[tb] = PE.n
                            on("tensor", _)

                            def _(e, tb=tb, dstbuf=dstbuf):
                                PE.wait(e, pe_t[tb])
                                e.tensor_copy(dstbuf[:, tb * 512:tb * 512 + 512],
                                              psP[tb % 2][:, :])
                                e.drain().then_inc(VE.h, 1)
                                VE.n += 1
                                ev_t[tb] = VE.n
                            on("vector", _)
                        return ev_t[nblocks - 1]

                    tok_kraw = proj(w_k, 8, Kr, tok_w[h], prev_ev if h else 0)

                    def _(e, tk=tok_kraw):
                        VE.wait(e, tk)
                        e.dma_start(Ksw[0:64, :], Kr[64:128, :]).then_inc(DW.h, 16)
                        e.dma_start(Ksw[64:128, :], Kr[0:64, :]).then_inc(DW.h, 16)
                        DW.n += 32
                    on("gpsimd", _)
                    tok_ksw = DW.n

                    def _(e, t=tok_ksw, tt=tok_tab_dw):
                        DW.wait(e, max(t, tt))
                        e.tensor_mul(Kr[:, :], Kr[:, :], t_cos[:, :])
                        vf(e, e.tensor_mul(Ksw[:, :], Ksw[:, :], t_sin[:, :]))
                        e.tensor_add(Kr[:, :], Kr[:, :], Ksw[:, :])
                        e.drain().then_inc(VE.h, 1)
                        VE.n += 1
                    on("vector", _)
                    tok_krope = VE.n

                    tok_qraw = proj(w_q, 4, Qr, 0, 0)

                    def _(e, tq=tok_qraw, tk=tok_krope):
                        VE.wait(e, max(tq, tk))
                        e.dma_start(Ksw[0:64, 0:T_OWN], Qr[64:128, :]).then_inc(DW.h, 16)
                        e.dma_start(Ksw[64:128, 0:T_OWN], Qr[0:64, :]).then_inc(DW.h, 16)
                        DW.n += 32
                    on("gpsimd", _)
                    tok_qsw = DW.n

                    def _(e, t=tok_qsw):
                        DW.wait(e, t)
                        e.tensor_mul(Qr[:, :], Qr[:, :], t_cos[:, 0:T_OWN])
                        vf(e, e.tensor_mul(Ksw[:, 0:T_OWN], Ksw[:, 0:T_OWN],
                                           t_sin[:, 0:T_OWN]))
                        e.tensor_add(Qr[:, :], Qr[:, :], Ksw[:, 0:T_OWN])
                        e.drain().then_inc(VE.h, 1)
                        VE.n += 1
                    on("vector", _)
                    tok_qrope = VE.n

                    tok_vt = proj(w_v, 8, Ksw, 0, tok_qrope)

                    if h + 1 < NH:
                        def _(e, hh=h + 1, tv=tok_vt):
                            VE.wait(e, tv)
                            e.dma_start(w_k[:, :, :], wk_r[hh]).then_inc(DW.h, 16)
                            e.dma_start(w_q[:, :, :], wq_r[hh]).then_inc(DW.h, 16)
                            e.dma_start(w_v[:, :, :], wv_r[hh]).then_inc(DW.h, 16)
                            DW.n += 48
                        on("gpsimd", _)
                        tok_w[h + 1] = DW.n

                    vtr_ev = [0] * 4
                    for grp in range(4):
                        def _(e, grp=grp, tv=tok_vt, pev=prev_ev):
                            if grp == 0:
                                VE.wait(e, tv)
                                if pev:
                                    VE.wait(e, pev)
                            else:
                                VE.wait(e, vtr_ev[grp - 1])
                            last = None
                            for i in range(8):
                                kt = grp * 8 + i
                                last = e.transpose(psMix[:, i * 128:i * 128 + 128],
                                                   Ksw[:, kt * 128:kt * 128 + 128],
                                                   g_idbf[:, :])
                            last.then_inc(PE.h, 1)
                            PE.n += 1
                        on("tensor", _)
                        tok_vtr = PE.n

                        def _(e, grp=grp, t=tok_vtr):
                            PE.wait(e, t)
                            for i in range(8):
                                kt = grp * 8 + i
                                e.tensor_copy(Vv[:, kt, :],
                                              psMix[:, i * 128:i * 128 + 128])
                            e.drain().then_inc(VE.h, 1)
                            VE.n += 1
                            vtr_ev[grp] = VE.n
                        on("vector", _)

                    tok_v = vtr_ev[3]

                    for g in range(4):
                        kblks = KBLKS_G[g]
                        ktiles = KTILES_G[g]
                        nkb = len(kblks)
                        nkt = len(ktiles)

                        qs_pe = [0] * (4 * nkb)
                        qs_se = [0] * (4 * nkb)
                        qs_ve = [0] * (4 * nkb)
                        tok_rowmm = 0
                        for i in range(4):
                            qt = g * 4 + i
                            for jb, kb in enumerate(kblks):
                                idx = i * nkb + jb
                                pi = idx % 2

                                def _(e, i=i, jb=jb, kb=kb, idx=idx, pi=pi, qt=qt,
                                      tq=tok_qrope, tv=tok_v):
                                    if idx == 0:
                                        VE.wait(e, max(tq, tv))
                                    if idx >= 2:
                                        SE.wait(e, qs_se[idx - 2])
                                    e.matmul(psP[pi][:, :],
                                             Qr[:, qt * 128:qt * 128 + 128],
                                             Kr[:, kb * 512:kb * 512 + 512],
                                             start=True, stop=True).then_inc(PE.h, 1)
                                    PE.n += 1
                                    qs_pe[idx] = PE.n
                                on("tensor", _)

                                def _(e, idx=idx, pi=pi):
                                    PE.wait(e, qs_pe[idx])
                                    if idx >= 2:
                                        VE.wait(e, qs_ve[idx - 2])
                                    e.activation(Pq[pi][:, :], psP[pi][:, :],
                                                 AF.Exp, scale=RS128)
                                    e.drain().then_inc(SE.h, 1)
                                    SE.n += 1
                                    qs_se[idx] = SE.n
                                on("scalar", _)

                                def _(e, idx=idx, pi=pi, jb=jb, kb=kb, qt=qt,
                                      tdx=tok_tab_dx):
                                    SE.wait(e, qs_se[idx])
                                    if idx == 0:
                                        DX.wait(e, tdx)
                                    e.scalar_tensor_tensor(
                                        Pq[pi][:, :],
                                        t_kbr[:, kb * 512:kb * 512 + 512],
                                        g_qpc[:, qt:qt + 1], Pq[pi][:, :],
                                        ALU.is_le, ALU.mult,
                                        accum_out=sums[:, jb:jb + 1]).then_inc(VE.h, 1)
                                    VE.n += 1
                                    qs_ve[idx] = VE.n
                                on("vector", _)

                            def _(e, i=i, nkb=nkb):
                                VE.wait(e, qs_ve[i * nkb + nkb - 1])
                                vf(e, e.reduce_sum(scol[:, 0:1], sums[:, 0:nkb], AX.X))
                                vf(e, e.reciprocal(scol[:, 1:2], scol[:, 0:1]))
                                e.tensor_copy(rs_bf[:, :], scol[:, 1:2])
                                e.drain().then_inc(VE.h, 1)
                                VE.n += 1
                            on("vector", _)
                            tok_rs = VE.n

                            def _(e, i=i, trs=tok_rs, tB=prev_B_se):
                                VE.wait(e, trs)
                                if i == 0 and tB:
                                    SE.wait(e, tB)
                                e.matmul(psB[0:1, i * 128:i * 128 + 128], rs_bf[:, :],
                                         g_idbf[:, :], start=True,
                                         stop=True).then_inc(PE.h, 1)
                                PE.n += 1
                            on("tensor", _)
                            tok_rowmm = PE.n

                        def _(e, t=tok_rowmm):
                            PE.wait(e, t)
                            e.tensor_copy(rs_row[0:1, :], psB[0:1, :])
                            e.drain().then_inc(VE.h, 1)
                            VE.n += 1
                        on("vector", _)
                        tok_rsrow = VE.n

                        def _(e, t=tok_rsrow):
                            VE.wait(e, t)
                            e.matmul(psB[:, :], g_ones[0:1, :], rs_row[0:1, :],
                                     start=True, stop=True).then_inc(PE.h, 1)
                            PE.n += 1
                        on("tensor", _)
                        tok_bmm = PE.n

                        def _(e, t=tok_bmm, pev=prev_ev):
                            PE.wait(e, t)
                            if pev:
                                VE.wait(e, pev)
                            e.copy(B_sb[:, :], psB[:, :])
                            e.drain().then_inc(SE.h, 1)
                            SE.n += 1
                        on("scalar", _)
                        tok_B = SE.n
                        prev_B_se = tok_B

                        ks_pe = [0] * nkt
                        ks_se = [0] * nkt
                        ks_ve = [0] * nkt
                        ks_pv = [0] * nkt
                        ai = g % 2
                        for jt, kt in enumerate(ktiles):
                            pi = jt % 2

                            def _(e, jt=jt, kt=kt, pi=pi, g=g, pev=prev_ev):
                                if jt == 0 and pev:
                                    VE.wait(e, pev)
                                if jt >= 2:
                                    SE.wait(e, ks_se[jt - 2])
                                e.matmul(psL[pi][:, :],
                                         Kr[:, kt * 128:kt * 128 + 128],
                                         Qr[:, g * 512:g * 512 + 512],
                                         start=True, stop=True).then_inc(PE.h, 1)
                                PE.n += 1
                                ks_pe[jt] = PE.n
                            on("tensor", _)

                            def _(e, jt=jt, pi=pi):
                                PE.wait(e, ks_pe[jt])
                                if jt >= 2:
                                    VE.wait(e, ks_ve[jt - 2])
                                    PE.wait(e, ks_pv[jt - 2])
                                e.activation(PT[pi][:, :], psL[pi][:, :], AF.Exp,
                                             scale=RS128)
                                e.drain().then_inc(SE.h, 1)
                                SE.n += 1
                                ks_se[jt] = SE.n
                            on("scalar", _)

                            def _(e, jt=jt, kt=kt, pi=pi, g=g):
                                SE.wait(e, ks_se[jt])
                                if jt >= 2:
                                    PE.wait(e, ks_pv[jt - 2])
                                e.scalar_tensor_tensor(
                                    PT[pi][:, :],
                                    t_qbc[:, g * 512:g * 512 + 512],
                                    g_kpc[:, kt:kt + 1], PT[pi][:, :],
                                    ALU.is_ge, ALU.mult)
                                e.drain().then_inc(VE.h, 1)
                                VE.n += 1
                                ks_ve[jt] = VE.n
                            on("vector", _)

                            def _(e, jt=jt, kt=kt, pi=pi, ai=ai, nkt=nkt):
                                VE.wait(e, ks_ve[jt])
                                e.matmul(psA[ai][:, :], Vv[:, kt, :], PT[pi][:, :],
                                         start=(jt == 0),
                                         stop=(jt == nkt - 1)).then_inc(PE.h, 1)
                                PE.n += 1
                                ks_pv[jt] = PE.n
                            on("tensor", _)

                        def _(e, g=g, ai=ai, tB=tok_B, tpv=ks_pv[nkt - 1],
                              pst=prev_st):
                            PE.wait(e, tpv)
                            SE.wait(e, tB)
                            if pst:
                                DS.wait(e, pst)
                            e.scalar_tensor_tensor(
                                aev[g % 2][:, :], psA[ai][:, :], 1.0, B_sb[:, :],
                                ALU.bypass, ALU.mult)
                            e.drain().then_inc(VE.h, 1)
                            VE.n += 1
                        on("vector", _)
                        prev_ev = VE.n

                        def _(e, h=h, g=g, t=prev_ev):
                            VE.wait(e, t)
                            e.dma_start(attnT_scr[h, :, g * 512:g * 512 + 512],
                                        aev[g % 2][:, :]).then_inc(DS.h, 16)
                            DS.n += 16
                        on("scalar", _)
                        prev_st = DS.n - 16

                    kr_free = PE.n
                tok_attn_stores = DS.n
            barrier()

            if K_STOP_PHASE == 2:
                def _(e, t=tok_attn_stores):
                    DS.wait(e, t)
                    e.dma_start(out.rearrange("(h p) q -> h p q", p=128),
                                attnT_scr[:, :, :]).then_inc(DW.h, 16)
                    DW.n += 16
                    e.wait_ge(DW.h, DW.n)
                on("gpsimd", _)
                return nc

            if K_STOP_PHASE == 21:
                def _(e):
                    VE.wait(e)
                    e.dma_start(out[0:128, :], Kr[:, 0:2048]).then_inc(DW.h, 16)
                    e.dma_start(out[128:256, :], Kr[:, 2048:4096]).then_inc(DW.h, 16)
                    e.dma_start(out[256:384, :], Qr[:, :]).then_inc(DW.h, 16)
                    e.dma_start(out[384:512, :].rearrange("p (a b) -> p a b", b=128),
                                Vv[:, 0:16, :]).then_inc(DW.h, 16)
                    e.dma_start(out[512:640, :].rearrange("p (a b) -> p a b", b=128),
                                Vv[:, 16:32, :]).then_inc(DW.h, 16)
                    DW.n += 80
                    e.wait_ge(DW.h, DW.n)
                on("gpsimd", _)
                return nc

        with ExitStack() as p3:
            attnT_all = p3.enter_context(
                nc.sbuf_tensor("attnT_all", [128, NH, T_OWN], BF16))
            wo_all = p3.enter_context(
                nc.sbuf_tensor("wo_all", [128, NH, D], BF16))
            oev = [p3.enter_context(nc.sbuf_tensor(f"oev{i}", [128, D], BF16))
                   for i in range(2)]
            psO = [p3.enter_context(nc.psum_tensor(f"psO{i}", [128, 512], F32))
                   for i in range(2)]

            def _(e, t=tok_attn_stores):
                DS.wait(e, t)
                for h in range(NH):
                    e.dma_start(attnT_all[:, h, :], attnT_scr[h]).then_inc(DX.h, 16)
                    DX.n += 16
            on("sync", _)
            tok_at = DX.n

            def _(e):
                for h in range(NH):
                    e.dma_start(wo_all[:, h, :], wo[h]).then_inc(DW.h, 16)
                    DW.n += 16
            on("gpsimd", _)
            tok_wo = DW.n

            o_pe = [0] * 64
            o_ev = [0] * 64
            oev_free = [(None, 0), (None, 0)]
            for i in range(16):
                bi = i % 2
                for db in range(4):
                    idx = i * 4 + db

                    def _(e, i=i, db=db, idx=idx, bi=bi, tat=tok_at, two=tok_wo,
                          rel=oev_free[bi]):
                        if idx == 0:
                            DX.wait(e, tat)
                            DW.wait(e, two)
                        if db == 0 and rel[0] is not None:
                            rel[0].wait(e, rel[1])
                        if idx >= 2:
                            VE.wait(e, o_ev[idx - 2])
                        bank = psO[db % 2]
                        last = None
                        for hh in range(NH):
                            last = e.matmul(bank[:, :],
                                            attnT_all[:, hh, i * 128:i * 128 + 128],
                                            wo_all[:, hh, db * 512:db * 512 + 512],
                                            start=(hh == 0), stop=(hh == NH - 1))
                        last.then_inc(PE.h, 1)
                        PE.n += 1
                        o_pe[idx] = PE.n
                    on("tensor", _)

                    def _(e, idx=idx, db=db, bi=bi):
                        PE.wait(e, o_pe[idx])
                        e.tensor_copy(oev[bi][:, db * 512:db * 512 + 512],
                                      psO[db % 2][:, :])
                        e.drain().then_inc(VE.h, 1)
                        VE.n += 1
                        o_ev[idx] = VE.n
                    on("vector", _)

                def _(e, i=i, bi=bi, t=None):
                    VE.wait(e, o_ev[i * 4 + 3])
                    e.dma_start(attn_out_scr[i], oev[bi][:, :]).then_inc(DS.h, 16)
                    DS.n += 16
                on("scalar", _)
                oev_free[bi] = (DS, DS.n)
            tok_oproj_stores = DS.n
        barrier()

        if K_STOP_PHASE == 3:
            def _(e, t=tok_oproj_stores):
                DS.wait(e, t)
                e.dma_start(out.rearrange("(i p) d -> i p d", p=128),
                            attn_out_scr[:, :, :]).then_inc(DW.h, 16)
                DW.n += 16
                e.wait_ge(DW.h, DW.n)
            on("gpsimd", _)
            return nc

        with ExitStack() as pr:
            def sbr(name, shape, dt):
                return pr.enter_context(nc.sbuf_tensor(name, list(shape), dt))
            gsb = sbr("gsb", (16, D), BF16)
            ebc = sbr("ebc", (16, 1), F32)
            gwT = sbr("gwT", (128, DC, 16), BF16)
            sc_e = sbr("sc_e", (16, T_OWN), F32)
            sfc_e = sbr("sfc_e", (16, T_OWN), F32)
            sc_t = sbr("sc_t", (128, 16, NE), F32)
            sfc_t = sbr("sfc_t", (128, 16, NE), F32)
            rtmp = sbr("rtmp", (128, 16, 32), F32)
            psR = [pr.enter_context(nc.psum_tensor(f"psR{i}", [128, 512], F32))
                   for i in range(4)]
            psRb = pr.enter_context(nc.psum_tensor("psRb", [128, 512], BF16))

            def _(e):
                e.dma_start(gsb[:, :], gate_w).then_inc(DW.h, 16)
                DW.n += 16
            on("gpsimd", _)
            tok_g = DW.n

            def _(e):
                e.dma_start(ebc[:, :], ebias).then_inc(DX.h, 16)
                DX.n += 16
            on("sync", _)
            tok_eb = DX.n

            rt_pe = [0] * DC
            rt_ev = [0] * DC
            for dc in range(DC):
                def _(e, dc=dc, t=tok_g):
                    if dc == 0:
                        DW.wait(e, t)
                    if dc >= 2:
                        VE.wait(e, rt_ev[dc - 2])
                    e.transpose(psRb[:, (dc % 2) * 16:(dc % 2) * 16 + 16],
                                gsb[:, dc * 128:dc * 128 + 128],
                                g_idbf[0:16, 0:16]).then_inc(PE.h, 1)
                    PE.n += 1
                    rt_pe[dc] = PE.n
                on("tensor", _)

                def _(e, dc=dc):
                    PE.wait(e, rt_pe[dc])
                    e.tensor_copy(gwT[:, dc, :],
                                  psRb[:, (dc % 2) * 16:(dc % 2) * 16 + 16])
                    e.drain().then_inc(VE.h, 1)
                    VE.n += 1
                    rt_ev[dc] = VE.n
                on("vector", _)
            tok_gwT = VE.n

            if K_RSTOP < 2:
                rl_pe = None
            rl_pe = [0] * 4
            for tb in range(K_RSTOP >= 2 and 4 or 0):
                def _(e, tb=tb, t=tok_gwT):
                    if tb == 0:
                        VE.wait(e, t)
                    last = None
                    for dc in range(DC):
                        last = e.matmul(psR[tb][0:16, :], gwT[:, dc, :],
                                        g_xnT[:, dc, tb * 512:tb * 512 + 512],
                                        start=(dc == 0), stop=(dc == DC - 1))
                    last.then_inc(PE.h, 1)
                    PE.n += 1
                    rl_pe[tb] = PE.n
                on("tensor", _)

            def _(e):
                last = None
                for tb in range(4):
                    PE.wait(e, rl_pe[tb])
                    last = e.activation(sc_e[:, tb * 512:tb * 512 + 512],
                                        psR[tb][0:16, :], AF.Sigmoid)
                e.drain().then_inc(SE.h, 1)
                SE.n += 1
            if K_RSTOP >= 2:
                on("scalar", _)
                SE_used = True
            else:
                SE.n -= 0
            tok_sig = SE.n

            def _(e, t=tok_sig, te=tok_eb):
                SE.wait(e, t)
                DX.wait(e, te)
                e.tensor_scalar(sfc_e[:, :], sc_e[:, :], ebc[:, :], None, ALU.add)
                e.drain().then_inc(VE.h, 1)
                VE.n += 1
            if K_RSTOP >= 2:
                on("vector", _)
                VE.n += 0
            tok_sfc = VE.n

            tr2_pe = [0] * 32
            tr2_ev = [0] * 32
            for k in range(K_RSTOP >= 3 and 32 or 0):
                srcbuf = sc_e if k < 16 else sfc_e
                dstbuf = sc_t if k < 16 else sfc_t
                i = k % 16

                def _(e, k=k, i=i, srcbuf=srcbuf, t=tok_sfc):
                    if k == 0:
                        VE.wait(e, t)
                    if k >= 2:
                        VE.wait(e, tr2_ev[k - 2])
                    e.transpose(psR[k % 2][:, 0:16], srcbuf[:, i * 128:i * 128 + 128],
                                g_idf32[0:16, 0:16]).then_inc(PE.h, 1)
                    PE.n += 1
                    tr2_pe[k] = PE.n
                on("tensor", _)

                def _(e, k=k, i=i, dstbuf=dstbuf):
                    PE.wait(e, tr2_pe[k])
                    e.tensor_copy(dstbuf[:, i, :], psR[k % 2][:, 0:16]).then_inc(VE.h, 1)
                    VE.n += 1
                    tr2_ev[k] = VE.n
                on("vector", _)
            tok_sct = VE.n

            def _(e, t=tok_sct):
                VE.wait(e, t)
                L = lambda k: rtmp[:, :, k]
                ms = [L(k) for k in range(16)]
                gs = [L(16 + k) for k in range(4)]
                gm = [L(20 + k) for k in range(4)]
                A = [L(24 + k) for k in range(4)]
                Bl = [L(28 + k) for k in range(4)]
                pairs = [(0, 1), (0, 2), (0, 3), (1, 2), (1, 3), (2, 3)]
                last = None
                for pi, (u, v) in enumerate(pairs):
                    for gi in range(4):
                        e.tensor_add(A[gi], sfc_t[:, :, 4 * gi + u],
                                     sfc_t[:, :, 4 * gi + v])
                    for gi in range(4):
                        if pi == 0:
                            last = e.tensor_copy(gs[gi], A[gi])
                        else:
                            last = e.tensor_max(gs[gi], gs[gi], A[gi])
                vf(e, last)
                for gi in range(4):
                    e.memset(Bl[gi], 0.0)
                for j in range(3):
                    for gi in range(4):
                        opp = [g for g in range(4) if g != gi][j]
                        e.tensor_tensor(A[gi], gs[opp], gs[gi], ALU.is_gt)
                    for gi in range(4):
                        e.tensor_add(Bl[gi], Bl[gi], A[gi])
                for gi in range(4):
                    last = e.tensor_scalar(gm[gi], Bl[gi], 1.0, None, ALU.is_le)
                vf(e, last)
                for gi in range(4):
                    e.tensor_scalar(A[gi], gm[gi], -1.0, 1e30, ALU.add, ALU.mult)
                for u in range(4):
                    for gi in range(4):
                        e.tensor_mul(ms[4 * gi + u], sfc_t[:, :, 4 * gi + u], gm[gi])
                    for gi in range(4):
                        last = e.tensor_add(ms[4 * gi + u], ms[4 * gi + u], A[gi])
                vf(e, last)
                e.nop().then_inc(VE.h, 1)
                VE.n += 1
            on("vector", _)

            def _(e):
                L = lambda k: rtmp[:, :, k]
                ms = [L(k) for k in range(16)]
                A = [L(24 + k) for k in range(4)]
                Bl = [L(28 + k) for k in range(4)]
                last = None
                for batch in range(4):
                    eis = [batch * 4 + si for si in range(4)]
                    for si in range(4):
                        e.memset(Bl[si], 0.0)
                    for j in range(15):
                        for si in range(4):
                            opp = [k for k in range(16) if k != eis[si]][j]
                            e.tensor_tensor(A[si], ms[opp], ms[eis[si]], ALU.is_gt)
                        for si in range(4):
                            e.tensor_add(Bl[si], Bl[si], A[si])
                    for si in range(4):
                        e.tensor_scalar(A[si], Bl[si], 3.0, None, ALU.is_le)
                    for si in range(4):
                        last = e.tensor_mul(g_w[:, :, eis[si]],
                                            sc_t[:, :, eis[si]], A[si])
                vf(e, last)
                sN = A + Bl
                for k in range(8):
                    e.tensor_add(sN[k], g_w[:, :, 2 * k], g_w[:, :, 2 * k + 1])
                tN = [L(16 + k) for k in range(4)]
                for k in range(4):
                    last = e.tensor_add(tN[k], sN[2 * k], sN[2 * k + 1])
                vf(e, last)
                e.tensor_add(L(20), tN[0], tN[1])
                vf(e, e.tensor_add(L(21), tN[2], tN[3]))
                vf(e, e.tensor_add(L(22), L(20), L(21)))
                vf(e, e.tensor_scalar(L(22), L(22), 1e-20, None, ALU.add))
                vf(e, e.reciprocal(L(23), L(22)))
                vf(e, e.tensor_scalar(L(22), L(23), 2.5, None, ALU.mult))
                for ei in range(NE):
                    e.tensor_mul(g_w[:, :, ei], g_w[:, :, ei], L(22))
                e.nop().then_inc(VE.h, 1)
                VE.n += 1
            on("vector", _)
        barrier()

        if K_STOP_PHASE == 4:
            def _(e):
                VE.wait(e)
                e.dma_start(out[0:128, 0:256].rearrange("p (a b) -> p a b", b=16),
                            g_w[:, :, :]).then_inc(DX.h, 16)
                DX.n += 16
                e.wait_ge(DX.h, DX.n)
            on("sync", _)
            return nc

        with ExitStack() as pw:
            natb = [pw.enter_context(nc.sbuf_tensor(f"natb{i}", [128, D], BF16))
                    for i in range(2)]
            dwnat = [pw.enter_context(nc.sbuf_tensor(f"dwnat{i}", [128, 8, 1024], BF16))
                     for i in range(2)]
            asm = [pw.enter_context(nc.sbuf_tensor(f"asm{i}", [128, 8, 8, 128], BF16))
                   for i in range(2)]
            trb = [pw.enter_context(nc.sbuf_tensor(f"trb{i}", [128, DC * 128], BF16))
                   for i in range(2)]
            psWt = [pw.enter_context(nc.psum_tensor(f"psWt{i}", [128, 1024], BF16))
                    for i in range(2)]

            gw_r = gw_moe.rearrange("e (f p) d -> e f p d", p=128)
            uw_r = uw_moe.rearrange("e (f p) d -> e f p d", p=128)
            dw_r = dw_moe.rearrange("e (h p) f -> e h p f", p=128)
            shg_r = sh_g.rearrange("(f p) d -> f p d", p=128)
            shu_r = sh_u.rearrange("(f p) d -> f p d", p=128)
            shd_r = sh_d.rearrange("(h p) f -> h p f", p=128)

            jobs = []
            for ee in range(NE):
                for fc in range(FC_MOE):
                    jobs.append((gw_r[ee, fc], DC, gwT_scr[ee, fc]))
                    jobs.append((uw_r[ee, fc], DC, uwT_scr[ee, fc]))
            for fc in range(FC_SH):
                jobs.append((shg_r[fc], DC, shgT_scr[fc]))
                jobs.append((shu_r[fc], DC, shuT_scr[fc]))


            if K_WJOBS is not None:
                jobs = jobs[:K_WJOBS]
            nat_free = [(None, 0), (None, 0)]
            trb_free = [(None, 0), (None, 0)]
            psw_free = (None, 0)
            for ji, (src, ndc, dst) in enumerate(jobs):
                bi = ji % 2

                def _(e, bi=bi, src=src, ndc=ndc, rel=nat_free[bi]):
                    if rel[0] is not None:
                        rel[0].wait(e, rel[1])
                    e.dma_start(natb[bi][:, 0:ndc * 128], src).then_inc(DW.h, 16)
                    DW.n += 16
                on("gpsimd", _)
                tok_ld = DW.n

                def _(e, bi=bi, ndc=ndc, t=tok_ld, rel=psw_free):
                    DW.wait(e, t)
                    if rel[0] is not None:
                        rel[0].wait(e, rel[1])
                    last = None
                    for dc in range(ndc):
                        bank = psWt[dc // 8]
                        last = e.transpose(
                            bank[:, (dc % 8) * 128:(dc % 8) * 128 + 128],
                            natb[bi][:, dc * 128:dc * 128 + 128],
                            g_idbf[:, :])
                    last.then_inc(PE.h, 1)
                    PE.n += 1
                on("tensor", _)
                tok_tr = PE.n
                nat_free[bi] = (PE, tok_tr)

                def _(e, bi=bi, ndc=ndc, t=tok_tr, rel=trb_free[bi]):
                    PE.wait(e, t)
                    if rel[0] is not None:
                        rel[0].wait(e, rel[1])
                    e.tensor_copy(trb[bi][:, 0:min(ndc, 8) * 128],
                                  psWt[0][:, 0:min(ndc, 8) * 128])
                    if ndc > 8:
                        e.tensor_copy(trb[bi][:, 1024:1024 + (ndc - 8) * 128],
                                      psWt[1][:, 0:(ndc - 8) * 128])
                    e.drain().then_inc(VE.h, 1)
                    VE.n += 1
                on("vector", _)
                tok_ev = VE.n
                psw_free = (VE, tok_ev)

                def _(e, bi=bi, ndc=ndc, dst=dst, t=tok_ev):
                    VE.wait(e, t)
                    src = trb[bi][:, 0:ndc * 128]
                    if len(dst.shape) == 3:
                        src = src.rearrange("p (f h) -> p f h", h=128)
                    e.dma_start(dst, src).then_inc(DS.h, 16)
                    DS.n += 16
                on("scalar", _)
                trb_free[bi] = (DS, DS.n)

            groups = []
            for ee in range(NE):
                for hh in range(2):
                    groups.append((
                        [dw_r[ee, hh * 8 + c] for c in range(8)],
                        [dwT_scr[ee, hh, fc] for fc in range(FC_MOE)]))
            for hh in range(2):
                for fhalf in range(2):
                    groups.append((
                        [shd_r[hh * 8 + c][:, fhalf * 1024:fhalf * 1024 + 1024]
                         for c in range(8)],
                        [shdT_scr[hh, fhalf * 8 + fc] for fc in range(8)]))

            nat2_free = [(None, 0), (None, 0)]
            asm_free = [(None, 0), (None, 0)]
            psw2_free = [(None, 0), (None, 0)]
            for gi2, (srcs, dsts) in enumerate(groups):
                bi = gi2 % 2

                def _(e, bi=bi, srcs=srcs, rel=nat2_free[bi]):
                    if rel[0] is not None:
                        rel[0].wait(e, rel[1])
                    for c in range(8):
                        e.dma_start(dwnat[bi][:, c, :], srcs[c]).then_inc(DW.h, 16)
                        DW.n += 16
                on("gpsimd", _)
                tok_ld2 = DW.n

                ev_last = 0
                for c in range(8):
                    def _(e, bi=bi, c=c, t=tok_ld2, rel=psw2_free[c % 2]):
                        if c == 0:
                            DW.wait(e, t)
                        if rel[0] is not None:
                            rel[0].wait(e, rel[1])
                        last = None
                        for fcj in range(8):
                            last = e.transpose(
                                psWt[c % 2][:, fcj * 128:fcj * 128 + 128],
                                dwnat[bi][:, c, fcj * 128:fcj * 128 + 128],
                                g_idbf[:, :])
                        last.then_inc(PE.h, 1)
                        PE.n += 1
                    on("tensor", _)
                    tok_tr2 = PE.n

                    def _(e, bi=bi, c=c, t=tok_tr2, rela=asm_free[bi]):
                        PE.wait(e, t)
                        if c == 0 and rela[0] is not None:
                            rela[0].wait(e, rela[1])
                        e.tensor_copy(asm[bi][:, :, c, :],
                                      psWt[c % 2][:, :].rearrange(
                                          "p (f h) -> p f h", h=128))
                        e.drain().then_inc(VE.h, 1)
                        VE.n += 1
                    on("vector", _)
                    psw2_free[c % 2] = (VE, VE.n)
                    ev_last = VE.n
                nat2_free[bi] = (PE, PE.n)

                def _(e, bi=bi, dsts=dsts, t=ev_last):
                    VE.wait(e, t)
                    for fc in range(8):
                        e.dma_start(dsts[fc],
                                    asm[bi][:, fc, :, :]).then_inc(DS.h, 16)
                        DS.n += 16
                on("scalar", _)
                asm_free[bi] = (DS, DS.n)
            tok_wprep = DS.n
        barrier()

        if K_STOP_PHASE == 5:
            def _(e, t=tok_wprep):
                DS.wait(e, t)
                e.dma_start(out[0:128, :].rearrange("p d -> p d"),
                            gwT_scr[0, 0]).then_inc(DW.h, 16)
                e.dma_start(out[128:256, 0:1024].rearrange("p (f h) -> p f h", h=128),
                            dwT_scr[0, 0, 0]).then_inc(DW.h, 16)
                DW.n += 32
                e.wait_ge(DW.h, DW.n)
            on("gpsimd", _)
            return nc

        with ExitStack() as pf:
            def sbf(name, shape, dt):
                return pf.enter_context(nc.sbuf_tensor(name, list(shape), dt))
            acc = sbf("acc", (128, 4, D), F32)
            actp = sbf("actp", (128, FC_MOE, TBS), BF16)
            gT = [sbf(f"gT{i}", (128, DC * 128), BF16) for i in range(2)]
            uT = [sbf(f"uT{i}", (128, DC * 128), BF16) for i in range(2)]
            gsil = sbf("gsil", (128, TBS), BF16)
            usb = sbf("usb", (128, TBS), BF16)
            dT = [sbf(f"dT{i}", (128, 8, 128), BF16) for i in range(2)]
            xrb = [sbf(f"xrb{i}", (128, D), F32) for i in range(2)]
            arb = [sbf(f"arb{i}", (128, D), BF16) for i in range(2)]
            orb = [sbf(f"orb{i}", (128, D), F32) for i in range(2)]
            psD = [pf.enter_context(nc.psum_tensor(f"psD{i}", [128, 1024], F32))
                   for i in range(4)]
            psG = [psD[0][:, 0:512], psD[0][:, 512:1024]]

            out_r = out.rearrange("(i p) d -> i p d", p=128)
            xsp_r = x_span.rearrange("(i p) d -> i p d", p=128)

            def _(e, t=max(tok_wprep, tok_oproj_stores)):
                DS.wait(e, t)
            on("sync", _)

            state = {
                "gT_free": [(None, 0), (None, 0)],
                "uT_free": [(None, 0), (None, 0)],
                "dT_free": [(None, 0), (None, 0)],
                "psG_free": (None, 0),
                "psD_free": [(None, 0)] * 4,
                "act_guard": 0,
                "act_tok": [0] * FC_MOE,
            }
            xrb_free = [(None, 0), (None, 0)]
            arb_free = [(None, 0), (None, 0)]
            orb_free = [(None, 0), (None, 0)]

            prev_act = [0]

            def emit_gu(tb, gsrc_fn, usrc_fn, nfc, slot_of):
                for fc in range(nfc):
                    bi = fc % 2
                    slot = slot_of(fc)

                    def _(e, bi=bi, gsrc=gsrc_fn(fc), usrc=usrc_fn(fc),
                          relg=state["gT_free"][bi], relu=state["uT_free"][bi]):
                        if relg[0] is not None:
                            relg[0].wait(e, relg[1])
                        e.dma_start(gT[bi][:, :], gsrc).then_inc(DX.h, 16)
                        DX.n += 16
                        if relu[0] is not None:
                            relu[0].wait(e, relu[1])
                        e.dma_start(uT[bi][:, :], usrc).then_inc(DX.h, 16)
                        DX.n += 16
                    on("sync", _)
                    tok_gu = DX.n

                    def _(e, tb=tb, bi=bi, t=tok_gu, rel=state["psG_free"],
                          reld=state["psD_free"][0]):
                        DX.wait(e, t)
                        if rel[0] is not None:
                            rel[0].wait(e, rel[1])
                        if reld[0] is not None:
                            reld[0].wait(e, reld[1])
                        for dc in range(DC):
                            e.matmul(psG[0][:, :], gT[bi][:, dc * 128:dc * 128 + 128],
                                     g_xnT[:, dc, tb * 512:tb * 512 + 512],
                                     start=(dc == 0), stop=(dc == DC - 1))
                        last = None
                        for dc in range(DC):
                            last = e.matmul(psG[1][:, :],
                                            uT[bi][:, dc * 128:dc * 128 + 128],
                                            g_xnT[:, dc, tb * 512:tb * 512 + 512],
                                            start=(dc == 0), stop=(dc == DC - 1))
                        last.then_inc(PE.h, 1)
                        PE.n += 1
                    on("tensor", _)
                    tok_gu_pe = PE.n
                    state["gT_free"][bi] = (PE, PE.n)
                    state["uT_free"][bi] = (PE, PE.n)

                    def _(e, t=tok_gu_pe, pa=prev_act[0]):
                        PE.wait(e, t)
                        if pa:
                            VE.wait(e, pa)
                        e.activation(gsil[:, :], psG[0][:, :], AF.Silu)
                        e.drain().then_inc(SE.h, 1)
                        SE.n += 1
                    on("scalar", _)
                    tok_sil = SE.n

                    def _(e, slot=slot, t=tok_gu_pe, ts=tok_sil,
                          guard=state["act_guard"]):
                        PE.wait(e, t)
                        vf(e, e.tensor_copy(usb[:, :], psG[1][:, :]))
                        SE.wait(e, ts)
                        if guard:
                            PE.wait(e, guard)
                        e.tensor_mul(actp[:, slot, :], gsil[:, :], usb[:, :])
                        e.drain().then_inc(VE.h, 1)
                        VE.n += 1
                    on("vector", _)
                    state["psG_free"] = (VE, VE.n)
                    state["act_tok"][slot] = VE.n
                    prev_act[0] = VE.n

            def emit_down(tb, dsrc_fn, weighted, ee):
                for hh in range(2):
                    dn_last = 0
                    for fc in range(FC_MOE):
                        bi = fc % 2

                        def _(e, bi=bi, src=dsrc_fn(hh, fc),
                              rel=state["dT_free"][bi]):
                            if rel[0] is not None:
                                rel[0].wait(e, rel[1])
                            e.dma_start(dT[bi][:, :, :], src).then_inc(DX.h, 16)
                            DX.n += 16
                        on("sync", _)
                        tok_d = DX.n

                        def _(e, fc=fc, bi=bi, t=tok_d,
                              ta=state["act_tok"][fc],
                              relps=list(state["psD_free"]),
                              relg=state["psG_free"]):
                            DX.wait(e, t)
                            VE.wait(e, ta)
                            if fc == 0:
                                for tt in range(4):
                                    r = relps[tt]
                                    if r[0] is not None:
                                        r[0].wait(e, r[1])
                                if relg[0] is not None:
                                    relg[0].wait(e, relg[1])
                            last = None
                            for tt in range(4):
                                e.matmul(psD[tt][:, 0:512],
                                         actp[:, fc, tt * 128:tt * 128 + 128],
                                         dT[bi][:, 0:4, :],
                                         start=(fc == 0), stop=(fc == FC_MOE - 1))
                                last = e.matmul(psD[tt][:, 512:1024],
                                                actp[:, fc, tt * 128:tt * 128 + 128],
                                                dT[bi][:, 4:8, :],
                                                start=(fc == 0),
                                                stop=(fc == FC_MOE - 1))
                            last.then_inc(PE.h, 1)
                            PE.n += 1
                        on("tensor", _)
                        dn_last = PE.n
                        state["dT_free"][bi] = (PE, PE.n)

                    for tt in range(4):
                        def _(e, tt=tt, hh=hh, tb=tb, ee=ee, weighted=weighted,
                              t=dn_last):
                            PE.wait(e, t)
                            dstap = acc[:, tt, hh * 1024:hh * 1024 + 1024]
                            if weighted:
                                e.scalar_tensor_tensor(
                                    dstap, psD[tt][:, :], g_w[:, tb * 4 + tt, ee:ee + 1],
                                    dstap, ALU.mult, ALU.add).then_inc(VE.h, 1)
                            else:
                                e.tensor_add(dstap, dstap,
                                             psD[tt][:, :]).then_inc(VE.h, 1)
                            VE.n += 1
                            state["psD_free"][tt] = (VE, VE.n)
                        on("vector", _)

            for tb in range(NTB):
                def _(e):
                    e.memset(acc[:, :, :], 0.0).then_inc(VE.h, 1)
                    VE.n += 1
                on("vector", _)

                for ee in range(K_FF_NE if K_FF_NE is not None else NE):
                    emit_gu(tb, lambda fc, ee=ee: gwT_scr[ee, fc],
                            lambda fc, ee=ee: uwT_scr[ee, fc],
                            FC_MOE, lambda fc: fc)
                    emit_down(tb, lambda hh, fc, ee=ee: dwT_scr[ee, hh, fc],
                              True, ee)
                    state["act_guard"] = PE.n

                for half in (range(2) if K_FF_SH else []):
                    emit_gu(tb, lambda fc, half=half: shgT_scr[half * 8 + fc],
                            lambda fc, half=half: shuT_scr[half * 8 + fc],
                            FC_MOE, lambda fc: fc)
                    emit_down(tb,
                              lambda hh, fc, half=half: shdT_scr[hh, half * 8 + fc],
                              False, 0)
                    state["act_guard"] = PE.n

                for tt in range(4):
                    i = tb * 4 + tt
                    bi = tt % 2

                    def _(e, i=i, bi=bi, relx=xrb_free[bi], rela=arb_free[bi]):
                        if relx[0] is not None:
                            relx[0].wait(e, relx[1])
                        e.dma_start(xrb[bi][:, :], xsp_r[i]).then_inc(DX.h, 16)
                        DX.n += 16
                        if rela[0] is not None:
                            rela[0].wait(e, rela[1])
                        e.dma_start(arb[bi][:, :], attn_out_scr[i]).then_inc(DX.h, 16)
                        DX.n += 16
                    on("sync", _)
                    tok_res = DX.n

                    def _(e, tt=tt, bi=bi, t=tok_res, rel=orb_free[bi]):
                        DX.wait(e, t)
                        if rel[0] is not None:
                            rel[0].wait(e, rel[1])
                        vf(e, e.tensor_add(orb[bi][:, :], xrb[bi][:, :], acc[:, tt, :]))
                        e.tensor_add(orb[bi][:, :], orb[bi][:, :], arb[bi][:, :])
                        e.drain().then_inc(VE.h, 1)
                        VE.n += 1
                    on("vector", _)
                    tok_o = VE.n
                    xrb_free[bi] = (VE, tok_o)
                    arb_free[bi] = (VE, tok_o)

                    def _(e, i=i, bi=bi, t=tok_o):
                        VE.wait(e, t)
                        e.dma_start(out_r[i], orb[bi][:, :]).then_inc(DS.h, 16)
                        DS.n += 16
                    on("scalar", _)
                    orb_free[bi] = (DS, DS.n)

            def _(e):
                e.wait_ge(DS.h, DS.n)
            on("sync", _)

    return nc


def _rope_tables(pos):
    frac = 2.0 * np.arange(64, dtype=np.float64) / 128.0
    ts = 10000.0 ** frac
    ang = pos[None, :].astype(np.float64) / ts[:, None]
    c = np.cos(ang)
    s = np.sin(ang)
    cos_t = np.concatenate([c, c], 0).astype(np.float32)
    sin_t = np.concatenate([-s, s], 0).astype(np.float32)
    return cos_t, sin_t


def kernel(L0_attn_x, L0_attn_wq, L0_attn_wk, L0_attn_wv, L0_attn_wo,
           L0_ffn_gate_weight, L0_ffn_e_score_correction_bias,
           L0_ffn_gate_w, L0_ffn_up_w, L0_ffn_down_w,
           L0_ffn_sh_gate_w, L0_ffn_sh_up_w, L0_ffn_sh_down_w):
    x = np.ascontiguousarray(np.asarray(L0_attn_x, dtype=np.float32))
    perm = np.concatenate([np.arange(0, 128, 2), np.arange(1, 128, 2)])
    wq_p = np.ascontiguousarray(np.asarray(L0_attn_wq)[:, :, perm])
    wk_p = np.ascontiguousarray(np.asarray(L0_attn_wk)[:, :, perm])
    wv = np.ascontiguousarray(np.asarray(L0_attn_wv))
    wo = np.ascontiguousarray(np.asarray(L0_attn_wo))
    gate = np.ascontiguousarray(np.asarray(L0_ffn_gate_weight))
    eb = np.ascontiguousarray(
        np.asarray(L0_ffn_e_score_correction_bias).reshape(NE, 1).astype(np.float32))
    gw = np.ascontiguousarray(np.asarray(L0_ffn_gate_w))
    uw = np.ascontiguousarray(np.asarray(L0_ffn_up_w))
    dw = np.ascontiguousarray(np.asarray(L0_ffn_down_w))
    shg = np.ascontiguousarray(np.asarray(L0_ffn_sh_gate_w))
    shu = np.ascontiguousarray(np.asarray(L0_ffn_sh_up_w))
    shd = np.ascontiguousarray(np.asarray(L0_ffn_sh_down_w))
    identity = np.eye(128, dtype=np.float32)
    ones_r = np.ones((1, 128), dtype=np.float32)

    in_maps = []
    core_meta = []
    for c in range(8):
        b, p = c // 2, c % 2
        if p == 0:
            own = [(0, 1024), (3072, 4096)]
            forn = [(1024, 2048), (2048, 3072)]
        else:
            own = [(1024, 2048), (2048, 3072)]
            forn = [(0, 1024), None]
        rows, pos = [], []
        for (a, bnd) in own:
            rows.append(x[b, a:bnd])
            pos.append(np.arange(a, bnd))
        for f in forn:
            if f is None:
                rows.append(np.zeros((1024, D), np.float32))
                pos.append(np.full(1024, BIGPOS))
            else:
                rows.append(x[b, f[0]:f[1]])
                pos.append(np.arange(f[0], f[1]))
        x_span = np.ascontiguousarray(np.concatenate(rows, 0))
        span_pos = np.concatenate(pos).astype(np.int64)
        own_pos = span_pos[:T_OWN]

        qpc = np.ascontiguousarray(own_pos.reshape(16, 128).T.astype(np.float32))
        kpc = np.ascontiguousarray(span_pos.reshape(32, 128).T.astype(np.float32))
        qbc = np.ascontiguousarray(
            np.broadcast_to(own_pos.astype(np.int16), (128, T_OWN)))
        kbr = np.ascontiguousarray(
            np.broadcast_to(span_pos.astype(np.int16), (128, SPAN)))
        tab_pos = np.where(span_pos >= BIGPOS, 0, span_pos)
        cos_t, sin_t = _rope_tables(tab_pos)

        in_maps.append({
            "x_span": x_span, "wq": wq_p, "wk": wk_p, "wv": wv, "wo": wo,
            "qpos_col": qpc, "kpos_col": kpc,
            "qpos_bc": qbc, "kpos_br": kbr,
            "cos_k": cos_t, "sin_ks": sin_t,
            "identity": identity, "ones_row": ones_r,
            "gate_w": gate, "ebias": eb,
            "gw_moe": gw, "uw_moe": uw, "dw_moe": dw,
            "sh_g": shg, "sh_u": shu, "sh_d": shd,
        })
        core_meta.append((b, own))

    nc = build_nc()
    res = run_bass_kernel_spmd(nc, in_maps, core_ids=list(range(8)))

    outp = np.zeros((B, S, D), np.float32)
    for c in range(8):
        b, own = core_meta[c]
        o = np.asarray(res.results[c]["out"])
        ofs = 0
        for (a, bnd) in own:
            n = bnd - a
            outp[b, a:bnd] = o[ofs:ofs + n]
            ofs += n
    return outp


# version 02 (final)
# speedup vs baseline: 1.1874x; 1.1874x over previous
import numpy as np
from contextlib import ExitStack

import concourse.bass as bass
import concourse.mybir as mybir
from concourse.bass_utils import run_bass_kernel_spmd

F32 = mybir.dt.float32
BF16 = mybir.dt.bfloat16
I16 = mybir.dt.int16
AF = mybir.ActivationFunctionType
ALU = mybir.AluOpType
AX = mybir.AxisListType

B, S, D = 4, 4096, 2048
NH, HD = 16, 128
DC = D // 128
T_OWN, SPAN = 2048, 4096
NE, F_MOE = 16, 1024
FC_MOE = F_MOE // 128
FC_SH = 2048 // 128
HC = 16
NTB, TBS = 4, 512
RS128 = float(1.0 / np.sqrt(128.0))
BIGPOS = 32000

K_STOP_PHASE = None
K_NH = None
K_RSTOP = 4
K_NOVF = False
K_WJOBS = None
K_FF_NE = None
K_FF_SH = True

KTILES_G = [
    list(range(0, 4)) + list(range(16, 24)),
    list(range(0, 8)) + list(range(16, 24)),
    list(range(0, 12)) + list(range(16, 32)),
    list(range(0, 32)),
]
KBLKS_G = [
    [0, 4, 5],
    [0, 1, 4, 5],
    [0, 1, 2, 4, 5, 6, 7],
    [0, 1, 2, 3, 4, 5, 6, 7],
]


def build_nc():
    nc = bass.Bass("TRN2", target_bir_lowering=False)

    def din(name, shape, dt=F32):
        return nc.dram_tensor(name, list(shape), dt, kind="ExternalInput").ap()

    x_span = din("x_span", (SPAN, D))
    wq = din("wq", (D, NH, HD))
    wk = din("wk", (D, NH, HD))
    wv = din("wv", (D, NH, HD))
    wo = din("wo", (NH, HD, D))
    qpos_col = din("qpos_col", (128, 16))
    kpos_col = din("kpos_col", (128, 32))
    qpos_bc = din("qpos_bc", (128, T_OWN), I16)
    kpos_br = din("kpos_br", (128, SPAN), I16)
    cos_k = din("cos_k", (128, SPAN))
    sin_ks = din("sin_ks", (128, SPAN))
    identity = din("identity", (128, 128))
    ones_row = din("ones_row", (1, 128))
    gate_w = din("gate_w", (NE, D))
    ebias = din("ebias", (NE, 1))
    gw_moe = din("gw_moe", (NE, F_MOE, D))
    uw_moe = din("uw_moe", (NE, F_MOE, D))
    dw_moe = din("dw_moe", (NE, D, F_MOE))
    sh_g = din("sh_g", (2048, D))
    sh_u = din("sh_u", (2048, D))
    sh_d = din("sh_d", (D, 2048))

    out = nc.dram_tensor("out", [T_OWN, D], F32, kind="ExternalOutput").ap()

    def scr(name, shape):
        return nc.dram_tensor(name, list(shape), BF16, kind="Internal").ap()

    attnT_scr = scr("attnT_scr", (NH, 128, T_OWN))
    attn_out_scr = scr("attn_out_scr", (16, 128, D))
    gwT_scr = scr("gwT_scr", (NE, FC_MOE, 128, DC * 128))
    uwT_scr = scr("uwT_scr", (NE, FC_MOE, 128, DC * 128))
    dwT_scr = scr("dwT_scr", (NE, 2, FC_MOE, 128, 8, 128))
    shgT_scr = scr("shgT_scr", (FC_SH, 128, DC * 128))
    shuT_scr = scr("shuT_scr", (FC_SH, 128, DC * 128))
    shdT_scr = scr("shdT_scr", (2, FC_SH, 128, 8, 128))

    es = ExitStack()
    with es:
        class Ch:
            __slots__ = ("h", "n")

            def __init__(self, name):
                self.h = es.enter_context(nc.semaphore(name))
                self.n = 0

            def inc(self, ins, k=1):
                ins.then_inc(self.h, k)
                self.n += k
                return self.n

            def wait(self, e, n=None):
                n = self.n if n is None else n
                if n > 0:
                    e.wait_ge(self.h, n)

        PE, VE, SE, GE = Ch("pe"), Ch("ve"), Ch("se"), Ch("ge")
        DX, DW, DS = Ch("dx"), Ch("dw"), Ch("ds")
        BAR = Ch("bar")

        blk = es.enter_context(nc.Block())

        def on(engname, fn):
            getattr(blk, engname)(fn)

        ENGS = ["tensor", "vector", "scalar", "gpsimd", "sync"]

        def barrier():
            for en, ch in (("vector", VE), ("scalar", SE), ("gpsimd", GE)):
                def _(e, ch=ch):
                    e.drain().then_inc(ch.h, 1)
                on(en, _)
            VE.n += 1
            SE.n += 1
            GE.n += 1
            for en in ENGS:
                def _(e):
                    for ch in (PE, VE, SE, GE, DX, DW, DS):
                        ch.wait(e)
                on(en, _)

        def vf(e, ins):
            if K_NOVF:
                return
            ins.then_inc(VE.h, 1)
            VE.n += 1
            e.wait_ge(VE.h, VE.n)

        g_idbf = es.enter_context(nc.sbuf_tensor("g_idbf", [128, 128], BF16))
        g_idf32 = es.enter_context(nc.sbuf_tensor("g_idf32", [128, 128], F32))
        g_ones = es.enter_context(nc.sbuf_tensor("g_ones", [1, 128], BF16))
        g_qpc = es.enter_context(nc.sbuf_tensor("g_qpc", [128, 16], F32))
        g_kpc = es.enter_context(nc.sbuf_tensor("g_kpc", [128, 32], F32))
        g_w = es.enter_context(nc.sbuf_tensor("g_w", [128, 16, NE], F32))
        g_xnT = es.enter_context(nc.sbuf_tensor("g_xnT", [128, DC, T_OWN], BF16))

        def _(e):
            e.dma_start(g_idbf[:, :], identity).then_inc(DW.h, 16)
            e.dma_start(g_ones[:, :], ones_row).then_inc(DW.h, 16)
            DW.n += 32
        on("gpsimd", _)
        tok_g0_dw = DW.n

        def _(e):
            e.dma_start(g_idf32[:, :], identity).then_inc(DX.h, 16)
            e.dma_start(g_qpc[:, :], qpos_col).then_inc(DX.h, 16)
            e.dma_start(g_kpc[:, :], kpos_col).then_inc(DX.h, 16)
            DX.n += 48
        on("sync", _)
        tok_g0_dx = DX.n

        tok_attn_stores = 0
        tok_oproj_stores = 0

        with ExitStack() as ascope:
            xnT_for = ascope.enter_context(
                nc.sbuf_tensor("xnT_for", [128, DC, T_OWN], BF16))

            def xnT_blk(dc, tb):
                if tb < 4:
                    return g_xnT[:, dc, tb * 512:tb * 512 + 512]
                return xnT_for[:, dc, (tb - 4) * 512:(tb - 4) * 512 + 512]

            with ExitStack() as p1:
                xb = [p1.enter_context(nc.sbuf_tensor(f"xb{i}", [128, D], F32))
                      for i in range(2)]
                dump = p1.enter_context(nc.sbuf_tensor("dump", [128, D], F32))
                xnt = [p1.enter_context(nc.sbuf_tensor(f"xnt{i}", [128, D], BF16))
                       for i in range(2)]
                lnS = p1.enter_context(nc.sbuf_tensor("lnS", [128, 8], F32))
                ps1 = [p1.enter_context(nc.psum_tensor(f"ps1_{i}", [128, 1024], BF16))
                       for i in range(2)]

                xt = x_span.rearrange("(j p) d -> j p d", p=128)
                xb_free = [(None, 0), (None, 0)]
                xnt_free = [(None, 0), (None, 0)]
                ps_free = (None, 0)
                prev_tok_var = 0
                for j in range(32):
                    bi = j % 2

                    def _(e, j=j, bi=bi, rel=xb_free[bi]):
                        if rel[0] is not None:
                            rel[0].wait(e, rel[1])
                        e.dma_start(xb[bi][:, :], xt[j]).then_inc(DX.h, 16)
                        DX.n += 16
                    on("sync", _)
                    tok_x = DX.n

                    def _(e, bi=bi, t=tok_x):
                        DX.wait(e, t)
                        e.reduce_sum(lnS[:, 0:1], xb[bi][:, :], AX.X)
                        vf(e, e.tensor_mul(dump[:, :], xb[bi][:, :], xb[bi][:, :]))
                        vf(e, e.reduce_sum(lnS[:, 1:2], dump[:, :], AX.X))
                        e.tensor_scalar(lnS[:, 2:3], lnS[:, 0:1], 1.0 / D, None, ALU.mult)
                        vf(e, e.tensor_scalar(lnS[:, 3:4], lnS[:, 1:2], 1.0 / D, None,
                                              ALU.mult))
                        vf(e, e.tensor_scalar(lnS[:, 4:5], lnS[:, 2:3], lnS[:, 2:3],
                                              -1e-5, ALU.mult, ALU.add))
                        e.tensor_sub(lnS[:, 5:6], lnS[:, 3:4], lnS[:, 4:5])
                        e.drain().then_inc(VE.h, 1)
                        VE.n += 1
                    on("vector", _)
                    tok_var = VE.n
                    prev_tok_var = tok_var

                    def _(e, tv=tok_var):
                        VE.wait(e, tv)
                        e.activation(lnS[:, 6:7], lnS[:, 5:6], AF.Ln)
                        e.activation(lnS[:, 7:8], lnS[:, 6:7], AF.Exp, scale=-0.5)
                        e.drain().then_inc(SE.h, 1)
                        SE.n += 1
                    on("scalar", _)
                    tok_rstd = SE.n

                    def _(e, bi=bi, tr=tok_rstd, rel=xnt_free[bi]):
                        SE.wait(e, tr)
                        if rel[0] is not None:
                            rel[0].wait(e, rel[1])
                        e.tensor_scalar(xnt[bi][:, :], xb[bi][:, :], lnS[:, 2:3],
                                        lnS[:, 7:8], ALU.subtract, ALU.mult)
                        e.drain().then_inc(VE.h, 1)
                        VE.n += 1
                    on("vector", _)
                    tok_xn = VE.n
                    xb_free[bi] = (VE, tok_xn)

                    def _(e, bi=bi, txn=tok_xn, rel=ps_free, tg=tok_g0_dw):
                        VE.wait(e, txn)
                        if rel[0] is not None:
                            rel[0].wait(e, rel[1])
                        DW.wait(e, tg)
                        last = None
                        for dc in range(16):
                            bank = ps1[dc // 8]
                            last = e.transpose(
                                bank[:, (dc % 8) * 128:(dc % 8) * 128 + 128],
                                xnt[bi][:, dc * 128:dc * 128 + 128],
                                g_idbf[:, :])
                        last.then_inc(PE.h, 1)
                        PE.n += 1
                    on("tensor", _)
                    tok_tr = PE.n
                    xnt_free[bi] = (PE, tok_tr)

                    dst = g_xnT if j < 16 else xnT_for
                    col = (j % 16) * 128

                    def _(e, ttr=tok_tr, dst=dst, col=col):
                        PE.wait(e, ttr)
                        e.tensor_copy(dst[:, 0:8, col:col + 128], ps1[0][:, :])
                        e.tensor_copy(dst[:, 8:16, col:col + 128], ps1[1][:, :])
                        e.drain().then_inc(VE.h, 1)
                        VE.n += 1
                    on("vector", _)
                    ps_free = (VE, VE.n)
            barrier()

            if K_STOP_PHASE == 1:
                def _(e):
                    VE.wait(e)
                    e.dma_start(out.rearrange("(c p) t -> p c t", p=128),
                                g_xnT[:, :, :]).then_inc(DW.h, 16)
                    DW.n += 16
                    e.wait_ge(DW.h, DW.n)
                on("gpsimd", _)
                return nc

            with ExitStack() as p2:
                def sb2(name, shape, dt):
                    return p2.enter_context(nc.sbuf_tensor(name, list(shape), dt))
                t_cos = sb2("t_cos", (128, SPAN), BF16)
                t_sin = sb2("t_sin", (128, SPAN), BF16)
                t_qbc = sb2("t_qbc", (128, T_OWN), I16)
                t_kbr = sb2("t_kbr", (128, SPAN), I16)
                w_q = sb2("w_q", (128, DC, HD), BF16)
                w_k = sb2("w_k", (128, DC, HD), BF16)
                w_v = sb2("w_v", (128, DC, HD), BF16)
                Kr = sb2("Kr", (128, SPAN), BF16)
                Ksw = sb2("Ksw", (128, SPAN), BF16)
                Vv = sb2("Vv", (128, 32, HD), BF16)
                Qr = sb2("Qr", (128, T_OWN), BF16)
                PT = [sb2(f"PT{i}", (128, 512), BF16) for i in range(2)]
                Pq = [sb2(f"Pq{i}", (128, 512), BF16) for i in range(2)]
                sums = sb2("sums", (128, 8), F32)
                scol = sb2("scol", (128, 2), F32)
                rs_bf = sb2("rs_bf", (128, 1), BF16)
                rs_row = sb2("rs_row", (1, 512), BF16)
                B_sb = sb2("B_sb", (128, 512), BF16)
                aev = [sb2(f"aev{i}", (128, 512), BF16) for i in range(2)]

                psP = [p2.enter_context(nc.psum_tensor(f"psP{i}", [128, 512], F32))
                       for i in range(2)]
                psL = [p2.enter_context(nc.psum_tensor(f"psL{i}", [128, 512], F32))
                       for i in range(2)]
                psA = [p2.enter_context(nc.psum_tensor(f"psA{i}", [128, 512], F32))
                       for i in range(2)]
                psB = p2.enter_context(nc.psum_tensor("psB", [128, 512], F32))
                psMix = p2.enter_context(nc.psum_tensor("psMix", [128, 1024], BF16))

                def _(e):
                    e.dma_start(t_cos[:, :], cos_k).then_inc(DW.h, 16)
                    e.dma_start(t_sin[:, :], sin_ks).then_inc(DW.h, 16)
                    DW.n += 32
                on("gpsimd", _)
                tok_tab_dw = DW.n

                def _(e):
                    e.dma_start(t_qbc[:, :], qpos_bc).then_inc(DX.h, 16)
                    e.dma_start(t_kbr[:, :], kpos_br).then_inc(DX.h, 16)
                    DX.n += 32
                on("sync", _)
                tok_tab_dx = DX.n

                wq_r = wq.rearrange("(c p) n h -> n p c h", p=128)
                wk_r = wk.rearrange("(c p) n h -> n p c h", p=128)
                wv_r = wv.rearrange("(c p) n h -> n p c h", p=128)

                tok_w = [0] * NH

                def _(e):
                    e.dma_start(w_k[:, :, :], wk_r[0]).then_inc(DW.h, 16)
                    e.dma_start(w_q[:, :, :], wq_r[0]).then_inc(DW.h, 16)
                    e.dma_start(w_v[:, :, :], wv_r[0]).then_inc(DW.h, 16)
                    DW.n += 48
                on("gpsimd", _)
                tok_w[0] = DW.n

                prev_ev = 0
                prev_st = 0
                prev_B_se = 0
                kr_free = 0

                for h in range(K_NH or NH):
                    def proj(wbuf, nblocks, dstbuf, first_wait_dw, first_wait_pe):
                        pe_t = [0] * nblocks
                        ev_t = [0] * nblocks
                        for tb in range(nblocks):
                            def _(e, tb=tb, wbuf=wbuf, fdw=first_wait_dw,
                                  fpe=first_wait_pe):
                                if tb == 0:
                                    if fdw:
                                        DW.wait(e, fdw)
                                    if fpe:
                                        VE.wait(e, fpe)
                                if tb >= 2:
                                    VE.wait(e, ev_t[tb - 2])
                                bank = psP[tb % 2]
                                last = None
                                for dc in range(DC):
                                    last = e.matmul(bank[:, :], wbuf[:, dc, :],
                                                    xnT_blk(dc, tb),
                                                    start=(dc == 0),
                                                    stop=(dc == DC - 1))
                                last.then_inc(PE.h, 1)
                                PE.n += 1
                                pe_t[tb] = PE.n
                            on("tensor", _)

                            def _(e, tb=tb, dstbuf=dstbuf):
                                PE.wait(e, pe_t[tb])
                                e.tensor_copy(dstbuf[:, tb * 512:tb * 512 + 512],
                                              psP[tb % 2][:, :])
                                e.drain().then_inc(VE.h, 1)
                                VE.n += 1
                                ev_t[tb] = VE.n
                            on("vector", _)
                        return ev_t[nblocks - 1]

                    tok_kraw = proj(w_k, 8, Kr, tok_w[h], prev_ev if h else 0)

                    def _(e, tk=tok_kraw):
                        VE.wait(e, tk)
                        e.dma_start(Ksw[0:64, :], Kr[64:128, :]).then_inc(DW.h, 16)
                        e.dma_start(Ksw[64:128, :], Kr[0:64, :]).then_inc(DW.h, 16)
                        DW.n += 32
                    on("gpsimd", _)
                    tok_ksw = DW.n

                    def _(e, t=tok_ksw, tt=tok_tab_dw):
                        DW.wait(e, max(t, tt))
                        e.tensor_mul(Kr[:, :], Kr[:, :], t_cos[:, :])
                        vf(e, e.tensor_mul(Ksw[:, :], Ksw[:, :], t_sin[:, :]))
                        e.tensor_add(Kr[:, :], Kr[:, :], Ksw[:, :])
                        e.drain().then_inc(VE.h, 1)
                        VE.n += 1
                    on("vector", _)
                    tok_krope = VE.n

                    tok_qraw = proj(w_q, 4, Qr, 0, 0)

                    def _(e, tq=tok_qraw, tk=tok_krope):
                        VE.wait(e, max(tq, tk))
                        e.dma_start(Ksw[0:64, 0:T_OWN], Qr[64:128, :]).then_inc(DW.h, 16)
                        e.dma_start(Ksw[64:128, 0:T_OWN], Qr[0:64, :]).then_inc(DW.h, 16)
                        DW.n += 32
                    on("gpsimd", _)
                    tok_qsw = DW.n

                    def _(e, t=tok_qsw):
                        DW.wait(e, t)
                        e.tensor_mul(Qr[:, :], Qr[:, :], t_cos[:, 0:T_OWN])
                        vf(e, e.tensor_mul(Ksw[:, 0:T_OWN], Ksw[:, 0:T_OWN],
                                           t_sin[:, 0:T_OWN]))
                        e.tensor_add(Qr[:, :], Qr[:, :], Ksw[:, 0:T_OWN])
                        e.drain().then_inc(VE.h, 1)
                        VE.n += 1
                    on("vector", _)
                    tok_qrope = VE.n

                    tok_vt = proj(w_v, 8, Ksw, 0, tok_qrope)

                    if h + 1 < NH:
                        def _(e, hh=h + 1, tv=tok_vt):
                            VE.wait(e, tv)
                            e.dma_start(w_k[:, :, :], wk_r[hh]).then_inc(DW.h, 16)
                            e.dma_start(w_q[:, :, :], wq_r[hh]).then_inc(DW.h, 16)
                            e.dma_start(w_v[:, :, :], wv_r[hh]).then_inc(DW.h, 16)
                            DW.n += 48
                        on("gpsimd", _)
                        tok_w[h + 1] = DW.n

                    vtr_ev = [0] * 4
                    for grp in range(4):
                        def _(e, grp=grp, tv=tok_vt, pev=prev_ev):
                            if grp == 0:
                                VE.wait(e, tv)
                                if pev:
                                    VE.wait(e, pev)
                            else:
                                VE.wait(e, vtr_ev[grp - 1])
                            last = None
                            for i in range(8):
                                kt = grp * 8 + i
                                last = e.transpose(psMix[:, i * 128:i * 128 + 128],
                                                   Ksw[:, kt * 128:kt * 128 + 128],
                                                   g_idbf[:, :])
                            last.then_inc(PE.h, 1)
                            PE.n += 1
                        on("tensor", _)
                        tok_vtr = PE.n

                        def _(e, grp=grp, t=tok_vtr):
                            PE.wait(e, t)
                            for i in range(8):
                                kt = grp * 8 + i
                                e.tensor_copy(Vv[:, kt, :],
                                              psMix[:, i * 128:i * 128 + 128])
                            e.drain().then_inc(VE.h, 1)
                            VE.n += 1
                            vtr_ev[grp] = VE.n
                        on("vector", _)

                    tok_v = vtr_ev[3]

                    for g in range(4):
                        kblks = KBLKS_G[g]
                        ktiles = KTILES_G[g]
                        nkb = len(kblks)
                        nkt = len(ktiles)

                        qs_pe = [0] * (4 * nkb)
                        qs_se = [0] * (4 * nkb)
                        qs_ve = [0] * (4 * nkb)
                        tok_rowmm = 0
                        for i in range(4):
                            qt = g * 4 + i
                            for jb, kb in enumerate(kblks):
                                idx = i * nkb + jb
                                pi = idx % 2

                                def _(e, i=i, jb=jb, kb=kb, idx=idx, pi=pi, qt=qt,
                                      tq=tok_qrope, tv=tok_v):
                                    if idx == 0:
                                        VE.wait(e, max(tq, tv))
                                    if idx >= 2:
                                        SE.wait(e, qs_se[idx - 2])
                                    e.matmul(psP[pi][:, :],
                                             Qr[:, qt * 128:qt * 128 + 128],
                                             Kr[:, kb * 512:kb * 512 + 512],
                                             start=True, stop=True).then_inc(PE.h, 1)
                                    PE.n += 1
                                    qs_pe[idx] = PE.n
                                on("tensor", _)

                                def _(e, idx=idx, pi=pi):
                                    PE.wait(e, qs_pe[idx])
                                    if idx >= 2:
                                        VE.wait(e, qs_ve[idx - 2])
                                    e.activation(Pq[pi][:, :], psP[pi][:, :],
                                                 AF.Exp, scale=RS128)
                                    e.drain().then_inc(SE.h, 1)
                                    SE.n += 1
                                    qs_se[idx] = SE.n
                                on("scalar", _)

                                def _(e, idx=idx, pi=pi, jb=jb, kb=kb, qt=qt,
                                      tdx=tok_tab_dx):
                                    SE.wait(e, qs_se[idx])
                                    if idx == 0:
                                        DX.wait(e, tdx)
                                    e.scalar_tensor_tensor(
                                        Pq[pi][:, :],
                                        t_kbr[:, kb * 512:kb * 512 + 512],
                                        g_qpc[:, qt:qt + 1], Pq[pi][:, :],
                                        ALU.is_le, ALU.mult,
                                        accum_out=sums[:, jb:jb + 1]).then_inc(VE.h, 1)
                                    VE.n += 1
                                    qs_ve[idx] = VE.n
                                on("vector", _)

                            def _(e, i=i, nkb=nkb):
                                VE.wait(e, qs_ve[i * nkb + nkb - 1])
                                vf(e, e.reduce_sum(scol[:, 0:1], sums[:, 0:nkb], AX.X))
                                vf(e, e.reciprocal(scol[:, 1:2], scol[:, 0:1]))
                                e.tensor_copy(rs_bf[:, :], scol[:, 1:2])
                                e.drain().then_inc(VE.h, 1)
                                VE.n += 1
                            on("vector", _)
                            tok_rs = VE.n

                            def _(e, i=i, trs=tok_rs, tB=prev_B_se):
                                VE.wait(e, trs)
                                if i == 0 and tB:
                                    SE.wait(e, tB)
                                e.matmul(psB[0:1, i * 128:i * 128 + 128], rs_bf[:, :],
                                         g_idbf[:, :], start=True,
                                         stop=True).then_inc(PE.h, 1)
                                PE.n += 1
                            on("tensor", _)
                            tok_rowmm = PE.n

                        def _(e, t=tok_rowmm):
                            PE.wait(e, t)
                            e.tensor_copy(rs_row[0:1, :], psB[0:1, :])
                            e.drain().then_inc(VE.h, 1)
                            VE.n += 1
                        on("vector", _)
                        tok_rsrow = VE.n

                        def _(e, t=tok_rsrow):
                            VE.wait(e, t)
                            e.matmul(psB[:, :], g_ones[0:1, :], rs_row[0:1, :],
                                     start=True, stop=True).then_inc(PE.h, 1)
                            PE.n += 1
                        on("tensor", _)
                        tok_bmm = PE.n

                        def _(e, t=tok_bmm, pev=prev_ev):
                            PE.wait(e, t)
                            if pev:
                                VE.wait(e, pev)
                            e.copy(B_sb[:, :], psB[:, :])
                            e.drain().then_inc(SE.h, 1)
                            SE.n += 1
                        on("scalar", _)
                        tok_B = SE.n
                        prev_B_se = tok_B

                        ks_pe = [0] * nkt
                        ks_se = [0] * nkt
                        ks_ve = [0] * nkt
                        ks_pv = [0] * nkt
                        ai = g % 2
                        for jt, kt in enumerate(ktiles):
                            pi = jt % 2

                            def _(e, jt=jt, kt=kt, pi=pi, g=g, pev=prev_ev):
                                if jt == 0 and pev:
                                    VE.wait(e, pev)
                                if jt >= 2:
                                    SE.wait(e, ks_se[jt - 2])
                                e.matmul(psL[pi][:, :],
                                         Kr[:, kt * 128:kt * 128 + 128],
                                         Qr[:, g * 512:g * 512 + 512],
                                         start=True, stop=True).then_inc(PE.h, 1)
                                PE.n += 1
                                ks_pe[jt] = PE.n
                            on("tensor", _)

                            def _(e, jt=jt, pi=pi):
                                PE.wait(e, ks_pe[jt])
                                if jt >= 2:
                                    VE.wait(e, ks_ve[jt - 2])
                                    PE.wait(e, ks_pv[jt - 2])
                                e.activation(PT[pi][:, :], psL[pi][:, :], AF.Exp,
                                             scale=RS128)
                                e.drain().then_inc(SE.h, 1)
                                SE.n += 1
                                ks_se[jt] = SE.n
                            on("scalar", _)

                            def _(e, jt=jt, kt=kt, pi=pi, g=g):
                                SE.wait(e, ks_se[jt])
                                if jt >= 2:
                                    PE.wait(e, ks_pv[jt - 2])
                                e.scalar_tensor_tensor(
                                    PT[pi][:, :],
                                    t_qbc[:, g * 512:g * 512 + 512],
                                    g_kpc[:, kt:kt + 1], PT[pi][:, :],
                                    ALU.is_ge, ALU.mult)
                                e.drain().then_inc(VE.h, 1)
                                VE.n += 1
                                ks_ve[jt] = VE.n
                            on("vector", _)

                            def _(e, jt=jt, kt=kt, pi=pi, ai=ai, nkt=nkt):
                                VE.wait(e, ks_ve[jt])
                                e.matmul(psA[ai][:, :], Vv[:, kt, :], PT[pi][:, :],
                                         start=(jt == 0),
                                         stop=(jt == nkt - 1)).then_inc(PE.h, 1)
                                PE.n += 1
                                ks_pv[jt] = PE.n
                            on("tensor", _)

                        def _(e, g=g, ai=ai, tB=tok_B, tpv=ks_pv[nkt - 1],
                              pst=prev_st):
                            PE.wait(e, tpv)
                            SE.wait(e, tB)
                            if pst:
                                DS.wait(e, pst)
                            e.scalar_tensor_tensor(
                                aev[g % 2][:, :], psA[ai][:, :], 1.0, B_sb[:, :],
                                ALU.bypass, ALU.mult)
                            e.drain().then_inc(VE.h, 1)
                            VE.n += 1
                        on("vector", _)
                        prev_ev = VE.n

                        def _(e, h=h, g=g, t=prev_ev):
                            VE.wait(e, t)
                            e.dma_start(attnT_scr[h, :, g * 512:g * 512 + 512],
                                        aev[g % 2][:, :]).then_inc(DS.h, 16)
                            DS.n += 16
                        on("scalar", _)
                        prev_st = DS.n - 16

                    kr_free = PE.n
                tok_attn_stores = DS.n
            barrier()

            if K_STOP_PHASE == 2:
                def _(e, t=tok_attn_stores):
                    DS.wait(e, t)
                    e.dma_start(out.rearrange("(h p) q -> h p q", p=128),
                                attnT_scr[:, :, :]).then_inc(DW.h, 16)
                    DW.n += 16
                    e.wait_ge(DW.h, DW.n)
                on("gpsimd", _)
                return nc

            if K_STOP_PHASE == 21:
                def _(e):
                    VE.wait(e)
                    e.dma_start(out[0:128, :], Kr[:, 0:2048]).then_inc(DW.h, 16)
                    e.dma_start(out[128:256, :], Kr[:, 2048:4096]).then_inc(DW.h, 16)
                    e.dma_start(out[256:384, :], Qr[:, :]).then_inc(DW.h, 16)
                    e.dma_start(out[384:512, :].rearrange("p (a b) -> p a b", b=128),
                                Vv[:, 0:16, :]).then_inc(DW.h, 16)
                    e.dma_start(out[512:640, :].rearrange("p (a b) -> p a b", b=128),
                                Vv[:, 16:32, :]).then_inc(DW.h, 16)
                    DW.n += 80
                    e.wait_ge(DW.h, DW.n)
                on("gpsimd", _)
                return nc

        with ExitStack() as p3:
            attnT_all = p3.enter_context(
                nc.sbuf_tensor("attnT_all", [128, NH, T_OWN], BF16))
            wo_all = p3.enter_context(
                nc.sbuf_tensor("wo_all", [128, NH, D], BF16))
            oev = [p3.enter_context(nc.sbuf_tensor(f"oev{i}", [128, D], BF16))
                   for i in range(2)]
            psO = [p3.enter_context(nc.psum_tensor(f"psO{i}", [128, 512], F32))
                   for i in range(2)]

            def _(e, t=tok_attn_stores):
                DS.wait(e, t)
                for h in range(NH):
                    e.dma_start(attnT_all[:, h, :], attnT_scr[h]).then_inc(DX.h, 16)
                    DX.n += 16
            on("sync", _)
            tok_at = DX.n

            def _(e):
                for h in range(NH):
                    e.dma_start(wo_all[:, h, :], wo[h]).then_inc(DW.h, 16)
                    DW.n += 16
            on("gpsimd", _)
            tok_wo = DW.n

            o_pe = [0] * 64
            o_ev = [0] * 64
            oev_free = [(None, 0), (None, 0)]
            for i in range(16):
                bi = i % 2
                for db in range(4):
                    idx = i * 4 + db

                    def _(e, i=i, db=db, idx=idx, bi=bi, tat=tok_at, two=tok_wo,
                          rel=oev_free[bi]):
                        if idx == 0:
                            DX.wait(e, tat)
                            DW.wait(e, two)
                        if db == 0 and rel[0] is not None:
                            rel[0].wait(e, rel[1])
                        if idx >= 2:
                            VE.wait(e, o_ev[idx - 2])
                        bank = psO[db % 2]
                        last = None
                        for hh in range(NH):
                            last = e.matmul(bank[:, :],
                                            attnT_all[:, hh, i * 128:i * 128 + 128],
                                            wo_all[:, hh, db * 512:db * 512 + 512],
                                            start=(hh == 0), stop=(hh == NH - 1))
                        last.then_inc(PE.h, 1)
                        PE.n += 1
                        o_pe[idx] = PE.n
                    on("tensor", _)

                    def _(e, idx=idx, db=db, bi=bi):
                        PE.wait(e, o_pe[idx])
                        e.tensor_copy(oev[bi][:, db * 512:db * 512 + 512],
                                      psO[db % 2][:, :])
                        e.drain().then_inc(VE.h, 1)
                        VE.n += 1
                        o_ev[idx] = VE.n
                    on("vector", _)

                def _(e, i=i, bi=bi, t=None):
                    VE.wait(e, o_ev[i * 4 + 3])
                    e.dma_start(attn_out_scr[i], oev[bi][:, :]).then_inc(DS.h, 16)
                    DS.n += 16
                on("scalar", _)
                oev_free[bi] = (DS, DS.n)
            tok_oproj_stores = DS.n
        barrier()

        if K_STOP_PHASE == 3:
            def _(e, t=tok_oproj_stores):
                DS.wait(e, t)
                e.dma_start(out.rearrange("(i p) d -> i p d", p=128),
                            attn_out_scr[:, :, :]).then_inc(DW.h, 16)
                DW.n += 16
                e.wait_ge(DW.h, DW.n)
            on("gpsimd", _)
            return nc

        with ExitStack() as pr:
            def sbr(name, shape, dt):
                return pr.enter_context(nc.sbuf_tensor(name, list(shape), dt))
            gsb = sbr("gsb", (16, D), BF16)
            ebc = sbr("ebc", (16, 1), F32)
            gwT = sbr("gwT", (128, DC, 16), BF16)
            sc_e = sbr("sc_e", (16, T_OWN), F32)
            sfc_e = sbr("sfc_e", (16, T_OWN), F32)
            sc_t = sbr("sc_t", (128, 16, NE), F32)
            sfc_t = sbr("sfc_t", (128, 16, NE), F32)
            rtmp = sbr("rtmp", (128, 16, 32), F32)
            psR = [pr.enter_context(nc.psum_tensor(f"psR{i}", [128, 512], F32))
                   for i in range(4)]
            psRb = pr.enter_context(nc.psum_tensor("psRb", [128, 512], BF16))

            def _(e):
                e.dma_start(gsb[:, :], gate_w).then_inc(DW.h, 16)
                DW.n += 16
            on("gpsimd", _)
            tok_g = DW.n

            def _(e):
                e.dma_start(ebc[:, :], ebias).then_inc(DX.h, 16)
                DX.n += 16
            on("sync", _)
            tok_eb = DX.n

            rt_pe = [0] * DC
            rt_ev = [0] * DC
            for dc in range(DC):
                def _(e, dc=dc, t=tok_g):
                    if dc == 0:
                        DW.wait(e, t)
                    if dc >= 2:
                        VE.wait(e, rt_ev[dc - 2])
                    e.transpose(psRb[:, (dc % 2) * 16:(dc % 2) * 16 + 16],
                                gsb[:, dc * 128:dc * 128 + 128],
                                g_idbf[0:16, 0:16]).then_inc(PE.h, 1)
                    PE.n += 1
                    rt_pe[dc] = PE.n
                on("tensor", _)

                def _(e, dc=dc):
                    PE.wait(e, rt_pe[dc])
                    e.tensor_copy(gwT[:, dc, :],
                                  psRb[:, (dc % 2) * 16:(dc % 2) * 16 + 16])
                    e.drain().then_inc(VE.h, 1)
                    VE.n += 1
                    rt_ev[dc] = VE.n
                on("vector", _)
            tok_gwT = VE.n

            if K_RSTOP < 2:
                rl_pe = None
            rl_pe = [0] * 4
            for tb in range(K_RSTOP >= 2 and 4 or 0):
                def _(e, tb=tb, t=tok_gwT):
                    if tb == 0:
                        VE.wait(e, t)
                    last = None
                    for dc in range(DC):
                        last = e.matmul(psR[tb][0:16, :], gwT[:, dc, :],
                                        g_xnT[:, dc, tb * 512:tb * 512 + 512],
                                        start=(dc == 0), stop=(dc == DC - 1))
                    last.then_inc(PE.h, 1)
                    PE.n += 1
                    rl_pe[tb] = PE.n
                on("tensor", _)

            def _(e):
                last = None
                for tb in range(4):
                    PE.wait(e, rl_pe[tb])
                    last = e.activation(sc_e[:, tb * 512:tb * 512 + 512],
                                        psR[tb][0:16, :], AF.Sigmoid)
                e.drain().then_inc(SE.h, 1)
                SE.n += 1
            if K_RSTOP >= 2:
                on("scalar", _)
                SE_used = True
            else:
                SE.n -= 0
            tok_sig = SE.n

            def _(e, t=tok_sig, te=tok_eb):
                SE.wait(e, t)
                DX.wait(e, te)
                e.tensor_scalar(sfc_e[:, :], sc_e[:, :], ebc[:, :], None, ALU.add)
                e.drain().then_inc(VE.h, 1)
                VE.n += 1
            if K_RSTOP >= 2:
                on("vector", _)
                VE.n += 0
            tok_sfc = VE.n

            tr2_pe = [0] * 32
            tr2_ev = [0] * 32
            for k in range(K_RSTOP >= 3 and 32 or 0):
                srcbuf = sc_e if k < 16 else sfc_e
                dstbuf = sc_t if k < 16 else sfc_t
                i = k % 16

                def _(e, k=k, i=i, srcbuf=srcbuf, t=tok_sfc):
                    if k == 0:
                        VE.wait(e, t)
                    if k >= 2:
                        VE.wait(e, tr2_ev[k - 2])
                    e.transpose(psR[k % 2][:, 0:16], srcbuf[:, i * 128:i * 128 + 128],
                                g_idf32[0:16, 0:16]).then_inc(PE.h, 1)
                    PE.n += 1
                    tr2_pe[k] = PE.n
                on("tensor", _)

                def _(e, k=k, i=i, dstbuf=dstbuf):
                    PE.wait(e, tr2_pe[k])
                    e.tensor_copy(dstbuf[:, i, :], psR[k % 2][:, 0:16]).then_inc(VE.h, 1)
                    VE.n += 1
                    tr2_ev[k] = VE.n
                on("vector", _)
            tok_sct = VE.n

            def _(e, t=tok_sct):
                VE.wait(e, t)
                L = lambda k: rtmp[:, :, k]
                ms = [L(k) for k in range(16)]
                gs = [L(16 + k) for k in range(4)]
                gm = [L(20 + k) for k in range(4)]
                A = [L(24 + k) for k in range(4)]
                Bl = [L(28 + k) for k in range(4)]
                pairs = [(0, 1), (0, 2), (0, 3), (1, 2), (1, 3), (2, 3)]
                last = None
                for pi, (u, v) in enumerate(pairs):
                    for gi in range(4):
                        e.tensor_add(A[gi], sfc_t[:, :, 4 * gi + u],
                                     sfc_t[:, :, 4 * gi + v])
                    for gi in range(4):
                        if pi == 0:
                            last = e.tensor_copy(gs[gi], A[gi])
                        else:
                            last = e.tensor_max(gs[gi], gs[gi], A[gi])
                vf(e, last)
                for gi in range(4):
                    e.memset(Bl[gi], 0.0)
                for j in range(3):
                    for gi in range(4):
                        opp = [g for g in range(4) if g != gi][j]
                        e.tensor_tensor(A[gi], gs[opp], gs[gi], ALU.is_gt)
                    for gi in range(4):
                        e.tensor_add(Bl[gi], Bl[gi], A[gi])
                for gi in range(4):
                    last = e.tensor_scalar(gm[gi], Bl[gi], 1.0, None, ALU.is_le)
                vf(e, last)
                for gi in range(4):
                    e.tensor_scalar(A[gi], gm[gi], -1.0, 1e30, ALU.add, ALU.mult)
                for u in range(4):
                    for gi in range(4):
                        e.tensor_mul(ms[4 * gi + u], sfc_t[:, :, 4 * gi + u], gm[gi])
                    for gi in range(4):
                        last = e.tensor_add(ms[4 * gi + u], ms[4 * gi + u], A[gi])
                vf(e, last)
                e.nop().then_inc(VE.h, 1)
                VE.n += 1
            on("vector", _)

            def _(e):
                L = lambda k: rtmp[:, :, k]
                ms = [L(k) for k in range(16)]
                A = [L(24 + k) for k in range(4)]
                Bl = [L(28 + k) for k in range(4)]
                last = None
                for batch in range(4):
                    eis = [batch * 4 + si for si in range(4)]
                    for si in range(4):
                        e.memset(Bl[si], 0.0)
                    for j in range(15):
                        for si in range(4):
                            opp = [k for k in range(16) if k != eis[si]][j]
                            e.tensor_tensor(A[si], ms[opp], ms[eis[si]], ALU.is_gt)
                        for si in range(4):
                            e.tensor_add(Bl[si], Bl[si], A[si])
                    for si in range(4):
                        e.tensor_scalar(A[si], Bl[si], 3.0, None, ALU.is_le)
                    for si in range(4):
                        last = e.tensor_mul(g_w[:, :, eis[si]],
                                            sc_t[:, :, eis[si]], A[si])
                vf(e, last)
                sN = A + Bl
                for k in range(8):
                    e.tensor_add(sN[k], g_w[:, :, 2 * k], g_w[:, :, 2 * k + 1])
                tN = [L(16 + k) for k in range(4)]
                for k in range(4):
                    last = e.tensor_add(tN[k], sN[2 * k], sN[2 * k + 1])
                vf(e, last)
                e.tensor_add(L(20), tN[0], tN[1])
                vf(e, e.tensor_add(L(21), tN[2], tN[3]))
                vf(e, e.tensor_add(L(22), L(20), L(21)))
                vf(e, e.tensor_scalar(L(22), L(22), 1e-20, None, ALU.add))
                vf(e, e.reciprocal(L(23), L(22)))
                vf(e, e.tensor_scalar(L(22), L(23), 2.5, None, ALU.mult))
                for ei in range(NE):
                    e.tensor_mul(g_w[:, :, ei], g_w[:, :, ei], L(22))
                e.nop().then_inc(VE.h, 1)
                VE.n += 1
            on("vector", _)
        barrier()

        if K_STOP_PHASE == 4:
            def _(e):
                VE.wait(e)
                e.dma_start(out[0:128, 0:256].rearrange("p (a b) -> p a b", b=16),
                            g_w[:, :, :]).then_inc(DX.h, 16)
                DX.n += 16
                e.wait_ge(DX.h, DX.n)
            on("sync", _)
            return nc

        with ExitStack() as pw:
            natb = [pw.enter_context(nc.sbuf_tensor(f"natb{i}", [128, D], BF16))
                    for i in range(2)]
            dwnat = [pw.enter_context(nc.sbuf_tensor(f"dwnat{i}", [128, 8, 1024], BF16))
                     for i in range(2)]
            asm = [pw.enter_context(nc.sbuf_tensor(f"asm{i}", [128, 8, 8, 128], BF16))
                   for i in range(2)]
            trb = [pw.enter_context(nc.sbuf_tensor(f"trb{i}", [128, DC * 128], BF16))
                   for i in range(2)]
            psWt = [pw.enter_context(nc.psum_tensor(f"psWt{i}", [128, 1024], BF16))
                    for i in range(2)]

            gw_r = gw_moe.rearrange("e (f p) d -> e f p d", p=128)
            uw_r = uw_moe.rearrange("e (f p) d -> e f p d", p=128)
            dw_r = dw_moe.rearrange("e (h p) f -> e h p f", p=128)
            shg_r = sh_g.rearrange("(f p) d -> f p d", p=128)
            shu_r = sh_u.rearrange("(f p) d -> f p d", p=128)
            shd_r = sh_d.rearrange("(h p) f -> h p f", p=128)

            jobs = []
            for ee in range(NE):
                for fc in range(FC_MOE):
                    jobs.append((gw_r[ee, fc], DC, gwT_scr[ee, fc]))
                    jobs.append((uw_r[ee, fc], DC, uwT_scr[ee, fc]))
            for fc in range(FC_SH):
                jobs.append((shg_r[fc], DC, shgT_scr[fc]))
                jobs.append((shu_r[fc], DC, shuT_scr[fc]))


            if K_WJOBS is not None:
                jobs = jobs[:K_WJOBS]
            nat_free = [(None, 0), (None, 0)]
            trb_free = [(None, 0), (None, 0)]
            psw_free = (None, 0)
            for ji, (src, ndc, dst) in enumerate(jobs):
                bi = ji % 2

                def _(e, bi=bi, src=src, ndc=ndc, rel=nat_free[bi]):
                    if rel[0] is not None:
                        rel[0].wait(e, rel[1])
                    e.dma_start(natb[bi][:, 0:ndc * 128], src).then_inc(DW.h, 16)
                    DW.n += 16
                on("gpsimd", _)
                tok_ld = DW.n

                def _(e, bi=bi, ndc=ndc, t=tok_ld, rel=psw_free):
                    DW.wait(e, t)
                    if rel[0] is not None:
                        rel[0].wait(e, rel[1])
                    last = None
                    for dc in range(ndc):
                        bank = psWt[dc // 8]
                        last = e.transpose(
                            bank[:, (dc % 8) * 128:(dc % 8) * 128 + 128],
                            natb[bi][:, dc * 128:dc * 128 + 128],
                            g_idbf[:, :])
                    last.then_inc(PE.h, 1)
                    PE.n += 1
                on("tensor", _)
                tok_tr = PE.n
                nat_free[bi] = (PE, tok_tr)

                def _(e, bi=bi, ndc=ndc, t=tok_tr, rel=trb_free[bi]):
                    PE.wait(e, t)
                    if rel[0] is not None:
                        rel[0].wait(e, rel[1])
                    e.tensor_copy(trb[bi][:, 0:min(ndc, 8) * 128],
                                  psWt[0][:, 0:min(ndc, 8) * 128])
                    if ndc > 8:
                        e.tensor_copy(trb[bi][:, 1024:1024 + (ndc - 8) * 128],
                                      psWt[1][:, 0:(ndc - 8) * 128])
                    e.drain().then_inc(VE.h, 1)
                    VE.n += 1
                on("vector", _)
                tok_ev = VE.n
                psw_free = (VE, tok_ev)

                def _(e, bi=bi, ndc=ndc, dst=dst, t=tok_ev):
                    VE.wait(e, t)
                    src = trb[bi][:, 0:ndc * 128]
                    if len(dst.shape) == 3:
                        src = src.rearrange("p (f h) -> p f h", h=128)
                    e.dma_start(dst, src).then_inc(DS.h, 16)
                    DS.n += 16
                on("scalar", _)
                trb_free[bi] = (DS, DS.n)

            groups = []
            for ee in range(NE):
                for hh in range(2):
                    groups.append((
                        [dw_r[ee, hh * 8 + c] for c in range(8)],
                        [dwT_scr[ee, hh, fc] for fc in range(FC_MOE)]))
            for hh in range(2):
                for fhalf in range(2):
                    groups.append((
                        [shd_r[hh * 8 + c][:, fhalf * 1024:fhalf * 1024 + 1024]
                         for c in range(8)],
                        [shdT_scr[hh, fhalf * 8 + fc] for fc in range(8)]))

            nat2_free = [(None, 0), (None, 0)]
            asm_free = [(None, 0), (None, 0)]
            psw2_free = [(None, 0), (None, 0)]
            for gi2, (srcs, dsts) in enumerate(groups):
                bi = gi2 % 2

                def _(e, bi=bi, srcs=srcs, rel=nat2_free[bi]):
                    if rel[0] is not None:
                        rel[0].wait(e, rel[1])
                    for c in range(8):
                        e.dma_start(dwnat[bi][:, c, :], srcs[c]).then_inc(DW.h, 16)
                        DW.n += 16
                on("gpsimd", _)
                tok_ld2 = DW.n

                ev_last = 0
                for c in range(8):
                    def _(e, bi=bi, c=c, t=tok_ld2, rel=psw2_free[c % 2]):
                        if c == 0:
                            DW.wait(e, t)
                        if rel[0] is not None:
                            rel[0].wait(e, rel[1])
                        last = None
                        for fcj in range(8):
                            last = e.transpose(
                                psWt[c % 2][:, fcj * 128:fcj * 128 + 128],
                                dwnat[bi][:, c, fcj * 128:fcj * 128 + 128],
                                g_idbf[:, :])
                        last.then_inc(PE.h, 1)
                        PE.n += 1
                    on("tensor", _)
                    tok_tr2 = PE.n

                    def _(e, bi=bi, c=c, t=tok_tr2, rela=asm_free[bi]):
                        PE.wait(e, t)
                        if c == 0 and rela[0] is not None:
                            rela[0].wait(e, rela[1])
                        e.tensor_copy(asm[bi][:, :, c, :],
                                      psWt[c % 2][:, :].rearrange(
                                          "p (f h) -> p f h", h=128))
                        e.drain().then_inc(VE.h, 1)
                        VE.n += 1
                    on("vector", _)
                    psw2_free[c % 2] = (VE, VE.n)
                    ev_last = VE.n
                nat2_free[bi] = (PE, PE.n)

                def _(e, bi=bi, dsts=dsts, t=ev_last):
                    VE.wait(e, t)
                    for fc in range(8):
                        e.dma_start(dsts[fc],
                                    asm[bi][:, fc, :, :]).then_inc(DS.h, 16)
                        DS.n += 16
                on("scalar", _)
                asm_free[bi] = (DS, DS.n)
            tok_wprep = DS.n
        barrier()

        if K_STOP_PHASE == 5:
            def _(e, t=tok_wprep):
                DS.wait(e, t)
                e.dma_start(out[0:128, :].rearrange("p d -> p d"),
                            gwT_scr[0, 0]).then_inc(DW.h, 16)
                e.dma_start(out[128:256, 0:1024].rearrange("p (f h) -> p f h", h=128),
                            dwT_scr[0, 0, 0]).then_inc(DW.h, 16)
                DW.n += 32
                e.wait_ge(DW.h, DW.n)
            on("gpsimd", _)
            return nc

        with ExitStack() as pf:
            def sbf(name, shape, dt):
                return pf.enter_context(nc.sbuf_tensor(name, list(shape), dt))
            acc = sbf("acc", (128, 4, D), F32)
            actp = sbf("actp", (128, FC_MOE, TBS), BF16)
            gT = [sbf(f"gT{i}", (128, DC * 128), BF16) for i in range(4)]
            uT = [sbf(f"uT{i}", (128, DC * 128), BF16) for i in range(4)]
            gsil = sbf("gsil", (128, TBS), BF16)
            usb = sbf("usb", (128, TBS), BF16)
            dT = [sbf(f"dT{i}", (128, 8, 128), BF16) for i in range(4)]
            xrb = [sbf(f"xrb{i}", (128, D), F32) for i in range(2)]
            arb = [sbf(f"arb{i}", (128, D), BF16) for i in range(2)]
            orb = [sbf(f"orb{i}", (128, D), F32) for i in range(2)]
            psD = [pf.enter_context(nc.psum_tensor(f"psD{i}", [128, 1024], F32))
                   for i in range(4)]
            psG = [psD[0][:, 0:512], psD[0][:, 512:1024]]

            out_r = out.rearrange("(i p) d -> i p d", p=128)
            xsp_r = x_span.rearrange("(i p) d -> i p d", p=128)

            def _(e, t=max(tok_wprep, tok_oproj_stores)):
                DS.wait(e, t)
            on("sync", _)

            state = {
                "gT_free": [(None, 0)] * 4,
                "uT_free": [(None, 0)] * 4,
                "dT_free": [(None, 0)] * 4,
                "psG_free": (None, 0),
                "psD_free": [(None, 0)] * 4,
                "act_guard": 0,
                "act_tok": [0] * FC_MOE,
            }
            xrb_free = [(None, 0), (None, 0)]
            arb_free = [(None, 0), (None, 0)]
            orb_free = [(None, 0), (None, 0)]

            prev_act = [0]

            def emit_gu(tb, gsrc_fn, usrc_fn, nfc, slot_of):
                for fc in range(nfc):
                    bi = fc % 4
                    slot = slot_of(fc)

                    def _(e, bi=bi, gsrc=gsrc_fn(fc), usrc=usrc_fn(fc),
                          relg=state["gT_free"][bi], relu=state["uT_free"][bi]):
                        if relg[0] is not None:
                            relg[0].wait(e, relg[1])
                        e.dma_start(gT[bi][:, :], gsrc).then_inc(DX.h, 16)
                        DX.n += 16
                        if relu[0] is not None:
                            relu[0].wait(e, relu[1])
                        e.dma_start(uT[bi][:, :], usrc).then_inc(DX.h, 16)
                        DX.n += 16
                    on("sync", _)
                    tok_gu = DX.n

                    def _(e, tb=tb, bi=bi, t=tok_gu, rel=state["psG_free"],
                          reld=state["psD_free"][0]):
                        DX.wait(e, t)
                        if rel[0] is not None:
                            rel[0].wait(e, rel[1])
                        if reld[0] is not None:
                            reld[0].wait(e, reld[1])
                        for dc in range(DC):
                            e.matmul(psG[0][:, :], gT[bi][:, dc * 128:dc * 128 + 128],
                                     g_xnT[:, dc, tb * 512:tb * 512 + 512],
                                     start=(dc == 0), stop=(dc == DC - 1))
                        last = None
                        for dc in range(DC):
                            last = e.matmul(psG[1][:, :],
                                            uT[bi][:, dc * 128:dc * 128 + 128],
                                            g_xnT[:, dc, tb * 512:tb * 512 + 512],
                                            start=(dc == 0), stop=(dc == DC - 1))
                        last.then_inc(PE.h, 1)
                        PE.n += 1
                    on("tensor", _)
                    tok_gu_pe = PE.n
                    state["gT_free"][bi] = (PE, PE.n)
                    state["uT_free"][bi] = (PE, PE.n)

                    def _(e, t=tok_gu_pe, pa=prev_act[0]):
                        PE.wait(e, t)
                        if pa:
                            VE.wait(e, pa)
                        e.activation(gsil[:, :], psG[0][:, :], AF.Silu)
                        e.drain().then_inc(SE.h, 1)
                        SE.n += 1
                    on("scalar", _)
                    tok_sil = SE.n

                    def _(e, slot=slot, t=tok_gu_pe, ts=tok_sil,
                          guard=state["act_guard"]):
                        PE.wait(e, t)
                        vf(e, e.tensor_copy(usb[:, :], psG[1][:, :]))
                        SE.wait(e, ts)
                        if guard:
                            PE.wait(e, guard)
                        e.tensor_mul(actp[:, slot, :], gsil[:, :], usb[:, :])
                        e.drain().then_inc(VE.h, 1)
                        VE.n += 1
                    on("vector", _)
                    state["psG_free"] = (VE, VE.n)
                    state["act_tok"][slot] = VE.n
                    prev_act[0] = VE.n

            def emit_down(tb, dsrc_fn, weighted, ee):
                for hh in range(2):
                    dn_last = 0
                    for fc in range(FC_MOE):
                        bi = (hh * FC_MOE + fc) % 4

                        def _(e, bi=bi, src=dsrc_fn(hh, fc),
                              rel=state["dT_free"][bi]):
                            if rel[0] is not None:
                                rel[0].wait(e, rel[1])
                            e.dma_start(dT[bi][:, :, :], src).then_inc(DW.h, 16)
                            DW.n += 16
                        on("gpsimd", _)
                        tok_d = DW.n

                        def _(e, fc=fc, bi=bi, t=tok_d,
                              ta=state["act_tok"][fc],
                              relps=list(state["psD_free"]),
                              relg=state["psG_free"]):
                            DW.wait(e, t)
                            VE.wait(e, ta)
                            if fc == 0:
                                for tt in range(4):
                                    r = relps[tt]
                                    if r[0] is not None:
                                        r[0].wait(e, r[1])
                                if relg[0] is not None:
                                    relg[0].wait(e, relg[1])
                            last = None
                            for tt in range(4):
                                e.matmul(psD[tt][:, 0:512],
                                         actp[:, fc, tt * 128:tt * 128 + 128],
                                         dT[bi][:, 0:4, :],
                                         start=(fc == 0), stop=(fc == FC_MOE - 1))
                                last = e.matmul(psD[tt][:, 512:1024],
                                                actp[:, fc, tt * 128:tt * 128 + 128],
                                                dT[bi][:, 4:8, :],
                                                start=(fc == 0),
                                                stop=(fc == FC_MOE - 1))
                            last.then_inc(PE.h, 1)
                            PE.n += 1
                        on("tensor", _)
                        dn_last = PE.n
                        state["dT_free"][bi] = (PE, PE.n)

                    for tt in range(4):
                        def _(e, tt=tt, hh=hh, tb=tb, ee=ee, weighted=weighted,
                              t=dn_last):
                            PE.wait(e, t)
                            dstap = acc[:, tt, hh * 1024:hh * 1024 + 1024]
                            if weighted:
                                e.scalar_tensor_tensor(
                                    dstap, psD[tt][:, :], g_w[:, tb * 4 + tt, ee:ee + 1],
                                    dstap, ALU.mult, ALU.add).then_inc(VE.h, 1)
                            else:
                                e.tensor_add(dstap, dstap,
                                             psD[tt][:, :]).then_inc(VE.h, 1)
                            VE.n += 1
                            state["psD_free"][tt] = (VE, VE.n)
                        on("vector", _)

            for tb in range(NTB):
                def _(e):
                    e.memset(acc[:, :, :], 0.0).then_inc(VE.h, 1)
                    VE.n += 1
                on("vector", _)

                for ee in range(K_FF_NE if K_FF_NE is not None else NE):
                    emit_gu(tb, lambda fc, ee=ee: gwT_scr[ee, fc],
                            lambda fc, ee=ee: uwT_scr[ee, fc],
                            FC_MOE, lambda fc: fc)
                    emit_down(tb, lambda hh, fc, ee=ee: dwT_scr[ee, hh, fc],
                              True, ee)
                    state["act_guard"] = PE.n

                for half in (range(2) if K_FF_SH else []):
                    emit_gu(tb, lambda fc, half=half: shgT_scr[half * 8 + fc],
                            lambda fc, half=half: shuT_scr[half * 8 + fc],
                            FC_MOE, lambda fc: fc)
                    emit_down(tb,
                              lambda hh, fc, half=half: shdT_scr[hh, half * 8 + fc],
                              False, 0)
                    state["act_guard"] = PE.n

                for tt in range(4):
                    i = tb * 4 + tt
                    bi = tt % 2

                    def _(e, i=i, bi=bi, relx=xrb_free[bi], rela=arb_free[bi]):
                        if relx[0] is not None:
                            relx[0].wait(e, relx[1])
                        e.dma_start(xrb[bi][:, :], xsp_r[i]).then_inc(DX.h, 16)
                        DX.n += 16
                        if rela[0] is not None:
                            rela[0].wait(e, rela[1])
                        e.dma_start(arb[bi][:, :], attn_out_scr[i]).then_inc(DX.h, 16)
                        DX.n += 16
                    on("sync", _)
                    tok_res = DX.n

                    def _(e, tt=tt, bi=bi, t=tok_res, rel=orb_free[bi]):
                        DX.wait(e, t)
                        if rel[0] is not None:
                            rel[0].wait(e, rel[1])
                        vf(e, e.tensor_add(orb[bi][:, :], xrb[bi][:, :], acc[:, tt, :]))
                        e.tensor_add(orb[bi][:, :], orb[bi][:, :], arb[bi][:, :])
                        e.drain().then_inc(VE.h, 1)
                        VE.n += 1
                    on("vector", _)
                    tok_o = VE.n
                    xrb_free[bi] = (VE, tok_o)
                    arb_free[bi] = (VE, tok_o)

                    def _(e, i=i, bi=bi, t=tok_o):
                        VE.wait(e, t)
                        e.dma_start(out_r[i], orb[bi][:, :]).then_inc(DS.h, 16)
                        DS.n += 16
                    on("scalar", _)
                    orb_free[bi] = (DS, DS.n)

            def _(e):
                e.wait_ge(DS.h, DS.n)
            on("sync", _)

    return nc


def _rope_tables(pos):
    frac = 2.0 * np.arange(64, dtype=np.float64) / 128.0
    ts = 10000.0 ** frac
    ang = pos[None, :].astype(np.float64) / ts[:, None]
    c = np.cos(ang)
    s = np.sin(ang)
    cos_t = np.concatenate([c, c], 0).astype(np.float32)
    sin_t = np.concatenate([-s, s], 0).astype(np.float32)
    return cos_t, sin_t


def kernel(L0_attn_x, L0_attn_wq, L0_attn_wk, L0_attn_wv, L0_attn_wo,
           L0_ffn_gate_weight, L0_ffn_e_score_correction_bias,
           L0_ffn_gate_w, L0_ffn_up_w, L0_ffn_down_w,
           L0_ffn_sh_gate_w, L0_ffn_sh_up_w, L0_ffn_sh_down_w):
    x = np.ascontiguousarray(np.asarray(L0_attn_x, dtype=np.float32))
    perm = np.concatenate([np.arange(0, 128, 2), np.arange(1, 128, 2)])
    wq_p = np.ascontiguousarray(np.asarray(L0_attn_wq)[:, :, perm])
    wk_p = np.ascontiguousarray(np.asarray(L0_attn_wk)[:, :, perm])
    wv = np.ascontiguousarray(np.asarray(L0_attn_wv))
    wo = np.ascontiguousarray(np.asarray(L0_attn_wo))
    gate = np.ascontiguousarray(np.asarray(L0_ffn_gate_weight))
    eb = np.ascontiguousarray(
        np.asarray(L0_ffn_e_score_correction_bias).reshape(NE, 1).astype(np.float32))
    gw = np.ascontiguousarray(np.asarray(L0_ffn_gate_w))
    uw = np.ascontiguousarray(np.asarray(L0_ffn_up_w))
    dw = np.ascontiguousarray(np.asarray(L0_ffn_down_w))
    shg = np.ascontiguousarray(np.asarray(L0_ffn_sh_gate_w))
    shu = np.ascontiguousarray(np.asarray(L0_ffn_sh_up_w))
    shd = np.ascontiguousarray(np.asarray(L0_ffn_sh_down_w))
    identity = np.eye(128, dtype=np.float32)
    ones_r = np.ones((1, 128), dtype=np.float32)

    in_maps = []
    core_meta = []
    for c in range(8):
        b, p = c // 2, c % 2
        if p == 0:
            own = [(0, 1024), (3072, 4096)]
            forn = [(1024, 2048), (2048, 3072)]
        else:
            own = [(1024, 2048), (2048, 3072)]
            forn = [(0, 1024), None]
        rows, pos = [], []
        for (a, bnd) in own:
            rows.append(x[b, a:bnd])
            pos.append(np.arange(a, bnd))
        for f in forn:
            if f is None:
                rows.append(np.zeros((1024, D), np.float32))
                pos.append(np.full(1024, BIGPOS))
            else:
                rows.append(x[b, f[0]:f[1]])
                pos.append(np.arange(f[0], f[1]))
        x_span = np.ascontiguousarray(np.concatenate(rows, 0))
        span_pos = np.concatenate(pos).astype(np.int64)
        own_pos = span_pos[:T_OWN]

        qpc = np.ascontiguousarray(own_pos.reshape(16, 128).T.astype(np.float32))
        kpc = np.ascontiguousarray(span_pos.reshape(32, 128).T.astype(np.float32))
        qbc = np.ascontiguousarray(
            np.broadcast_to(own_pos.astype(np.int16), (128, T_OWN)))
        kbr = np.ascontiguousarray(
            np.broadcast_to(span_pos.astype(np.int16), (128, SPAN)))
        tab_pos = np.where(span_pos >= BIGPOS, 0, span_pos)
        cos_t, sin_t = _rope_tables(tab_pos)

        in_maps.append({
            "x_span": x_span, "wq": wq_p, "wk": wk_p, "wv": wv, "wo": wo,
            "qpos_col": qpc, "kpos_col": kpc,
            "qpos_bc": qbc, "kpos_br": kbr,
            "cos_k": cos_t, "sin_ks": sin_t,
            "identity": identity, "ones_row": ones_r,
            "gate_w": gate, "ebias": eb,
            "gw_moe": gw, "uw_moe": uw, "dw_moe": dw,
            "sh_g": shg, "sh_u": shu, "sh_d": shd,
        })
        core_meta.append((b, own))

    nc = build_nc()
    res = run_bass_kernel_spmd(nc, in_maps, core_ids=list(range(8)))

    outp = np.zeros((B, S, D), np.float32)
    for c in range(8):
        b, own = core_meta[c]
        o = np.asarray(res.results[c]["out"])
        ofs = 0
        for (a, bnd) in own:
            n = bnd - a
            outp[b, a:bnd] = o[ofs:ofs + n]
            ofs += n
    return outp
